# version 04
# baseline (speedup 1.0000x reference)
import numpy as np
import concourse.bass as bass
import concourse.mybir as mybir
from concourse.bass_utils import run_bass_kernel_spmd
from concourse.library_overlay import lower_extended_insts
import concourse.dve_ops as dve_ops
import concourse.dve_spec as dve_spec
from concourse.dve_op import DveOp
from concourse.dve_spec import Spec, Src0, Src1, C0, C1

F32 = mybir.dt.float32
BF16 = mybir.dt.bfloat16
ALU = mybir.AluOpType
ACT = mybir.ActivationFunctionType
AXL = mybir.AxisListType

NCORE = 8
D = 1024
HD = 4096
KC = 8
FCH = 32
TOWN = 2048
LH = 160
RH = 2
TE = LH + TOWN + RH
RC1 = float(1.5 * 2.0 ** 23)
RC2 = float(1.5 * 2.0 ** 52)
GA = 224
GB1 = 316
GB2 = 184
PA_MAXG = None
SKIP = set()
EPS_LN = 1e-5
LN127 = float(np.log(127.0))

_OPS = {}


def _register_ops():
    if _OPS:
        return _OPS

    from concourse.dve_ops_raw import _compute_dtype

    def mk(name, spec, reference):
        for op in dve_ops.OPS:
            if op.name == name:
                _OPS[name] = op
                return op
        shas = {}
        for ver in ("v3", "v4"):
            r = dve_spec.lower(spec, ver=ver, perf_en=False, name=name)
            shas[ver] = r.sha(ver)
        op = DveOp(name, spec, subdim=False, uops_sha=shas,
                   reference=reference)
        victim = next(i for i, o in enumerate(dve_ops.OPS)
                      if o.name.startswith("DVEHASH32L"))
        vrow = dve_ops._SUB_OPCODE_FOR_NAME[dve_ops.OPS[victim].name]
        del dve_ops._SUB_OPCODE_FOR_NAME[dve_ops.OPS[victim].name]
        dve_ops.OPS[victim] = op
        dve_ops._SUB_OPCODE_FOR_NAME[name] = vrow
        dve_ops.CUSTOM_DVE_REFS[name] = op.reference
        dve_ops.CUSTOM_DVE_SPECS[name] = spec
        _OPS[name] = op
        return op

    _m = Src0 * Src1
    _u = (_m + C0) - C0
    _w = ((_m - _u) + C1) - C1
    mk("ANT_SCALE_ROUND", Spec(body=_u + _w),
       lambda in0, in1, s0, s1, imm2: np.round(
           in0.astype(_compute_dtype(in0)) * in1))
    mk("ANT_AXPY2", Spec(body=Src0 * C0 + Src1 * C1),
       lambda in0, in1, s0, s1, imm2:
           in0.astype(_compute_dtype(in0)) * s0 + in1 * s1)
    return _OPS


def _grp_ranges(start, end, step):
    return [(a, min(a + step, end)) for a in range(start, end, step)]


def sap(h, pattern, offset=0):
    return bass.AP(h, offset, pattern)


def build(debug=False, phases=("P0", "PA", "PB1", "PB2", "PC")):
    ops = _register_ops()
    nc = bass.Bass("TRN2", target_bir_lowering=False)

    din = {}

    def dri(name, shape, dt=F32):
        din[name] = nc.dram_tensor(name, list(shape), dt, kind="ExternalInput")

    dri("xfm", (KC, 128, TE))
    dri("w1t", (KC, 128, HD))
    dri("w2t", (FCH, 128, D))
    for nm in ("wk", "wv", "wr", "wo", "wfr"):
        dri(nm, (KC, 128, D))
    dri("wfk", (KC, 128, HD))
    dri("wfv", (FCH, 128, D))
    dri("convw", (128, 3 * FCH))
    dri("convb", (128, FCH))
    for nm in ("ln1g", "ln1b", "td", "mixk", "mixv", "mixr", "ln2g",
               "ln2b", "tfv"):
        dri(nm, (128, KC))
    for nm in ("pen", "xp0", "eml", "emr"):
        dri(nm, (128, 1))
    dri("ident", (128, 128))

    d_out = nc.dram_tensor("out", [KC, 128, TOWN], F32, kind="ExternalOutput")
    ddbg = {}
    if debug:
        for nm in ("dxa", "dx1", "dxb"):
            ddbg[nm] = nc.dram_tensor(nm, [KC, 128, TE], F32,
                                      kind="ExternalOutput")
        ddbg["dsc"] = nc.dram_tensor("dsc", [128, 16], F32,
                                     kind="ExternalOutput")

    scr = {
        "xa": nc.dram_tensor("s_xa", [KC, 128, TE], F32, kind="Internal"),
        "x1": nc.dram_tensor("s_x1", [KC, 128, TE], F32, kind="Internal"),
        "xb": nc.dram_tensor("s_xb", [KC, 128, TE], F32, kind="Internal"),
        "w1q": nc.dram_tensor("s_w1q", [KC, 128, HD], BF16, kind="Internal"),
        "w2q": nc.dram_tensor("s_w2q", [FCH, 128, D], BF16, kind="Internal"),
    }

    from contextlib import ExitStack
    with ExitStack() as glob:
        SB = lambda name, shape, dt=F32: glob.enter_context(
            nc.sbuf_tensor(name, list(shape), dt))
        env = {
            "nc": nc, "din": din, "scr": scr, "d_out": d_out, "ddbg": ddbg,
            "ops": ops, "debug": debug,
            "ident": SB("g_ident", (128, 128)),
            "ones": SB("g_ones", (128, 128)),
            "onesbf": SB("g_onesbf", (33, 128), BF16),
            "convw": SB("g_convw", (128, 3 * FCH)),
            "convb": SB("g_convb", (128, FCH)),
        }
        for nm in ("ln1g", "ln1b", "td", "mixk", "mixv", "mixr", "ln2g",
                   "ln2b", "tfv", "ew", "omk", "omv", "omr"):
            env[nm] = SB("g_" + nm, (128, KC))
        for nm in ("pen", "xp0", "eml", "emr", "eps", "ln127", "q1c", "q2c"):
            env[nm] = SB("g_" + nm, (128, 1))

        def _reset():
            nc.reset(previous_sbuf_base=nc.sbuf_base,
                     previous_sbuf_top=nc.sbuf_top,
                     previous_psum_base=nc.psum_base,
                     previous_psum_top=nc.psum_top,
                     previous_local_dram_base=nc.local_dram_base,
                     previous_shared_dram_base=nc.shared_dram_base)

        if "P0" in phases:
            _emit_p0(env)
            _reset()
        if "PA" in phases:
            _emit_ffn(env, stage="A")
            _reset()
        if "PB1" in phases:
            _emit_pb1(env)
            _reset()
        if "PB2" in phases:
            _emit_pb2(env)
            _reset()
        if "PC" in phases:
            _emit_ffn(env, stage="C")
            _reset()
        _emit_tail(env, phases)
    return nc


def _emit_p0(env):
    nc = env["nc"]
    din = env["din"]
    scr = env["scr"]
    SCR = env["ops"]["ANT_SCALE_ROUND"]
    from contextlib import ExitStack
    with ExitStack() as ctx:
        SB = lambda name, shape, dt=F32: ctx.enter_context(
            nc.sbuf_tensor(name, list(shape), dt))
        sem = lambda name: ctx.enter_context(nc.semaphore(name))
        wstg = SB("p0_wstg", (128, 2 * HD))
        qstg = SB("p0_qstg", (128, 2 * HD), BF16)
        aw = SB("p0_aw", (128, FCH))
        srow = SB("p0_srow", (1, 256))
        scl = SB("p0_scl", (1, 8))
        hlv = SB("p0_hlv", (1, 8), BF16)
        scbc = SB("p0_scbc", (128, HD))
        s1bc = SB("p0_s1bc", (128, 1))
        s2bc = SB("p0_s2bc", (128, 1))
        p_tr = ctx.enter_context(nc.psum_tensor("p0_ptr", [128, 512], F32))
        sldc = [sem("ldc0"), sem("ldc1")]
        swp = [sem("wp%d" % i) for i in range(4)]
        sq2 = [sem("q2%d" % i) for i in range(4)]
        sstp = [sem("st%d" % i) for i in range(4)]
        sv = sem("v")
        sfold = sem("fold")
        strp = sem("trp")
        ssc = sem("sc")
        sbc = sem("bc")
        sscb = sem("scb")
        sqd = sem("qd")
        s127 = sem("s127")
        snk = sem("snk")
        NW = KC + FCH

        def wsrc(i):
            return (din["w1t"].ap()[i, :, :] if i < KC
                    else din["w2t"].ap()[i - KC, :, :])

        def wwid(i):
            return HD if i < KC else D

        with nc.Block("p0") as blk:
            @blk.sync
            def _(sync):
                consts = ([("ident", env["ident"]), ("convw", env["convw"]),
                           ("convb", env["convb"])]
                          + [(nm, env[nm]) for nm in
                             ("ln1g", "ln1b", "td", "mixk", "mixv", "mixr",
                              "ln2g", "ln2b", "tfv", "pen", "xp0", "eml",
                              "emr")])
                nconst = [0, 0]
                for j, (nm, dst) in enumerate(consts):
                    sync.dma_start(out=dst[:], in_=din[nm].ap()
                                   ).then_inc(sldc[j % 2], 16)
                    nconst[j % 2] += 1
                for i in range(NW):
                    if i >= 2:
                        sync.wait_ge(sv, i - 1)
                    sync.dma_start(
                        out=wstg[:, (i % 2) * HD:(i % 2) * HD + wwid(i)],
                        in_=wsrc(i)).then_inc(swp[i % 4], 16)
                for i in range(NW):
                    if i < 2:
                        sync.wait_ge(sv, NW)
                    else:
                        sync.wait_ge(sqd, i - 1)
                    sync.dma_start(
                        out=wstg[:, (i % 2) * HD:(i % 2) * HD + wwid(i)],
                        in_=wsrc(i)).then_inc(sq2[i % 4], 16)

            @blk.vector
            def _(vector):
                vector.memset(env["ones"][:], 1.0)
                vector.memset(env["onesbf"][:, :], 1.0)
                vector.memset(env["eps"][:], EPS_LN)
                vector.memset(env["ln127"][:], 127.0
                              ).then_inc(s127, 1)
                vector.wait_ge(sldc[0], 16 * 8)
                vector.wait_ge(sldc[1], 16 * 8)
                vector.tensor_scalar(env["omk"][:], env["mixk"][:],
                                     -1.0, 1.0, ALU.mult, ALU.add)
                vector.tensor_scalar(env["omv"][:], env["mixv"][:],
                                     -1.0, 1.0, ALU.mult, ALU.add)
                vector.tensor_scalar(env["omr"][:], env["mixr"][:],
                                     -1.0, 1.0, ALU.mult, ALU.add)
                for wi in range(2):
                    nchunk = KC if wi == 0 else FCH
                    base = 0 if wi == 0 else KC
                    for c in range(nchunk):
                        i = base + c
                        vector.wait_ge(swp[i % 4], 16 * (i // 4 + 1))
                        vector.tensor_reduce(
                            aw[:, c:c + 1],
                            wstg[:, (i % 2) * HD:(i % 2) * HD + wwid(i)],
                            AXL.X, ALU.add, apply_absolute_value=True
                        ).then_inc(sv, 1)
                    vector.drain()
                    vector.tensor_reduce(aw[:, 0:1], aw[:, 0:nchunk],
                                         AXL.X, ALU.add).then_inc(sfold, 1)
                    vector.wait_ge(strp, wi + 1)
                    vector.tensor_copy(srow[0:1, 0:128], p_tr[0:1, 0:128])
                    vector.drain()
                    vector.tensor_reduce(scl[0:1, 0:1], srow[0:1, 0:128],
                                         AXL.X, ALU.add)
                    vector.drain()
                    vector.tensor_scalar(scl[0:1, 1:2], scl[0:1, 0:1],
                                         1.0 / (D * HD), EPS_LN,
                                         ALU.mult, ALU.max)
                    vector.drain()
                    vector.reciprocal(scl[0:1, 2:3], scl[0:1, 1:2])
                    vector.drain()
                    vector.tensor_copy(hlv[0:1, 0:2], scl[0:1, 1:3])
                    vector.drain()
                    vector.scalar_tensor_tensor(
                        hlv[0:1, 2:4], scl[0:1, 1:3], 0.0, hlv[0:1, 0:2],
                        ALU.add, ALU.subtract).then_inc(ssc, 1)
                    vector.wait_ge(sbc, wi + 1)
                    if wi == 0:
                        vector.tensor_scalar(env["q1c"][:], p_tr[:, 0:1],
                                             1.0 / 127.0, None, ALU.mult,
                                             ALU.bypass)
                    else:
                        vector.tensor_scalar(env["q2c"][:], p_tr[:, 0:1],
                                             0.5 / 127.0, None, ALU.mult,
                                             ALU.bypass)
                    vector.tensor_copy((s1bc if wi == 0 else s2bc)[:],
                                       p_tr[:, 1:2]).then_inc(sscb, 1)
                for i in range(NW):
                    wid = wwid(i)
                    slot = wstg[:, (i % 2) * HD:(i % 2) * HD + wid]
                    qdst = qstg[:, (i % 2) * HD:(i % 2) * HD + wid]
                    if i == 0:
                        vector.memset(scbc[:], 1.0)
                        vector.drain()
                        vector.tensor_scalar(scbc[:], scbc[:], s1bc[:, 0:1],
                                             None, ALU.mult, ALU.bypass)
                        vector.drain()
                    if i == KC:
                        vector.memset(scbc[:, 0:D], 1.0)
                        vector.drain()
                        vector.tensor_scalar(scbc[:, 0:D], scbc[:, 0:D],
                                             s2bc[:, 0:1], None, ALU.mult,
                                             ALU.bypass)
                        vector.drain()
                    vector.wait_ge(sq2[i % 4], 16 * (i // 4 + 1))
                    if i >= 2:
                        vector.wait_ge(sstp[(i - 2) % 4],
                                       16 * ((i - 2) // 4 + 1))
                    vector._custom_dve(SCR, out=qdst, in0=slot,
                                       in1=scbc[:, 0:wid], s0=RC2, s1=RC1)
                    vector.drain()
                    vector.tensor_scalar(qdst, qdst, -1.0, 1.0,
                                         ALU.max, ALU.min).then_inc(sqd, 1)

            @blk.tensor
            def _(tensor):
                for wi in range(2):
                    tensor.wait_ge(sfold, wi + 1)
                    tensor.transpose(p_tr[0:1, 0:128], aw[:, 0:1],
                                     env["ident"][:]).then_inc(strp, 1)
                    tensor.wait_ge(ssc, wi + 1)
                    tensor.matmul(p_tr[:, 0:2], env["onesbf"][0:1, :],
                                  hlv[0:1, 0:2], start=True, stop=False)
                    tensor.matmul(p_tr[:, 0:2], env["onesbf"][0:1, :],
                                  hlv[0:1, 2:4], start=False, stop=True
                                  ).then_inc(sbc, 1)

            @blk.scalar
            def _(scalar):
                scalar.wait_ge(s127, 1)
                scalar.activation(env["ln127"][:], env["ln127"][:], ACT.Ln)
                scalar.drain()
                scalar.wait_ge(sldc[0], 16 * 8)
                scalar.wait_ge(sldc[1], 16 * 8)
                scalar.activation(env["ew"][:], env["td"][:], ACT.Exp)
                scalar.drain()
                scalar.activation(env["ew"][:], env["ew"][:], ACT.Exp,
                                  scale=-1.0)

            @blk.gpsimd
            def _(gpsimd):
                for i in range(NW):
                    wid = wwid(i)
                    gpsimd.wait_ge(sqd, i + 1)
                    dst = (scr["w1q"].ap()[i, :, :] if i < KC
                           else scr["w2q"].ap()[i - KC, :, :])
                    gpsimd.dma_start(
                        out=dst, in_=qstg[:, (i % 2) * HD:(i % 2) * HD + wid]
                    ).then_inc(sstp[i % 4], 16)
                for p in range(4):
                    n = len([i for i in range(NW) if i % 4 == p])
                    gpsimd.wait_ge(sstp[p], 16 * n)


def _emit_ffn(env, stage):
    nc = env["nc"]
    din = env["din"]
    scr = env["scr"]
    ddbg = env["ddbg"]
    debug = env["debug"]
    SCR = env["ops"]["ANT_SCALE_ROUND"]
    AX2 = env["ops"]["ANT_AXPY2"]
    g_ones, g_onesbf, g_ident = env["ones"], env["onesbf"], env["ident"]
    g_convw, g_convb = env["convw"], env["convb"]

    if stage == "A":
        groups = _grp_ranges(0, TE, GA)
        dram_out = scr["xa"]
    else:
        groups = _grp_ranges(LH, LH + TOWN, GA)
        dram_out = env["d_out"]
    if PA_MAXG is not None:
        groups = groups[:PA_MAXG]
    NG = len(groups)

    from contextlib import ExitStack
    with ExitStack() as ctx:
        pfx = "f" + stage
        SB = lambda name, shape, dt=F32: ctx.enter_context(
            nc.sbuf_tensor(pfx + name, list(shape), dt))
        PS = lambda name, shape: ctx.enter_context(
            nc.psum_tensor(pfx + name, list(shape), F32))
        sem = lambda name: ctx.enter_context(nc.semaphore(pfx + name))

        GW = GA + 2
        w1q = SB("w1q", (128, KC * HD), BF16)
        w2q = SB("w2q", (128, FCH * D), BF16)
        x_g = SB("x", (128, 2 * KC * GW))
        z_g = SB("z", (128, KC * GW))
        xq_g = SB("xq", (128, KC * GW), BF16)
        h_g = SB("h", (128, FCH * GA))
        dq_t = SB("dq", (128, 2 * GW))
        sq_t = dq_t
        ctp = SB("ct", (128, 2 * GW))
        ct2 = dq_t
        hq_t = SB("hq", (128, 2 * GA), BF16)
        xa_t = SB("xo", (128, KC * GA))
        m_bc = SB("mbc", (128, GW))
        rstd = SB("rst", (128, GW))
        gam = SB("gam", (128, GW))
        u_sb = SB("usb", (128, GW))
        q1 = SB("q1", (128, GW))
        u2_sb = u_sb
        q2 = m_bc
        gamh = gam
        rm = SB("rm", (128, GW))
        amx = SB("amx", (128, 4))
        row4 = SB("r4", (4, 128))
        arow = SB("arw", (1, 512))
        hlr = SB("hlr", (1, 512), BF16)
        hlr2 = SB("hl2", (1, 512), BF16)
        var_t = q1

        p_misc = PS("pm", (128, 1024))
        p_h = PS("ph", (128, 1024))
        p_y = PS("py", (128, 2048))

        def pdc(dc, n):
            if dc < 4:
                return p_y[:, dc * 512:dc * 512 + n]
            if dc < 6:
                return p_h[:, (dc - 4) * 512:(dc - 4) * 512 + n]
            return p_misc[:, (dc - 6) * 512:(dc - 6) * 512 + n]

        sxg = [sem("sxg%d" % i) for i in range(NG)]
        sxasg = [sem("sxas%d" % i) for i in range(NG)]
        swlp = [sem("swlp%d" % i) for i in range(4)]
        ssq = sem("ssq"); ssmm = sem("ssmm")
        svar = sem("svar"); srs = sem("srs")
        srm = sem("srm"); stp = sem("stp"); sam = sem("sam"); st2 = sem("st2")
        sr3 = sem("sr3"); srw = sem("srw"); shl = sem("shl"); sbc = sem("sbc")
        sga = sem("sga"); sxq = sem("sxq")
        sm1p = [sem("sm1a"), sem("sm1b")]
        scvp = [sem("scva"), sem("scvb")]
        sa2p = [sem("sa2a"), sem("sa2b")]
        sslp = [sem("ssla"), sem("sslb")]
        shqp = [sem("shqa"), sem("shqb")]
        sm2p = [sem("sm2a"), sem("sm2b")]
        srmh = sem("srmh")
        stph = sem("stph"); samh = sem("samh"); st2h = sem("st2h")
        sr3h = sem("sr3h"); srwh = sem("srwh"); shlh = sem("shlh")
        sbch = sem("sbch"); sgh = sem("sgh")
        sm2d = sem("sm2d"); sxa = sem("sxa")
        sp127 = sem("sp127")
        snk = sem("snk")

        def grange(g):
            g0, g1 = groups[g]
            a0, a1 = max(g0 - 1, 0), min(g1 + 1, TE)
            return g0, g1, a0, a1

        with nc.Block(pfx) as blk:
            @blk.sync
            def _(sync):
                if "wload" not in SKIP:
                    for i in range(KC):
                        sync.dma_start(out=w1q[:, i * HD:(i + 1) * HD],
                                       in_=scr["w1q"].ap()[i, :, :]
                                       ).then_inc(swlp[i % 4], 16)
                    for j in range(FCH):
                        sync.dma_start(out=w2q[:, j * D:(j + 1) * D],
                                       in_=scr["w2q"].ap()[j, :, :]
                                       ).then_inc(swlp[(KC + j) % 4], 16)
                for g in range(NG):
                    g0, g1, a0, a1 = grange(g)
                    if g >= 2:
                        sync.wait_ge(sxa, KC * (g - 1))
                    for c in range(KC):
                        src = (din["xfm"].ap()[c, :, a0:a1] if stage == "A"
                               else scr["xb"].ap()[c, :, a0:a1])
                        sync.dma_start(
                            out=x_g[:, ((g % 2) * KC + c) * GW:
                                    ((g % 2) * KC + c) * GW + (a1 - a0)],
                            in_=src).then_inc(sxg[g], 16)

            @blk.vector
            def _(vector):
                vector.memset(amx[:], 0.0)
                for g in range(NG):
                    g0, g1, a0, a1 = grange(g)
                    w = a1 - a0
                    gw = g1 - g0
                    xb = (g % 2) * KC * GW
                    nt = (w + 127) // 128
                    nth = (gw + 127) // 128
                    vector.wait_ge(sxg[g], 16 * KC)
                    for c in ([] if "stats" in SKIP else range(KC)):
                        if c >= 2:
                            vector.wait_ge(ssmm, KC * g + c - 1)
                        vector.tensor_tensor(
                            sq_t[:, (c % 2) * GW:(c % 2) * GW + w],
                            x_g[:, xb + c * GW:xb + c * GW + w],
                            x_g[:, xb + c * GW:xb + c * GW + w],
                            ALU.mult).then_inc(ssq, 1)
                    if g >= 1 and "seng" not in SKIP:
                        vector.wait_ge(srs, g)
                    if "stats" not in SKIP:
                        vector.wait_ge(ssmm, KC * (g + 1))
                        vector.tensor_scalar(m_bc[:, 0:w], p_misc[:, 0:w],
                                             1.0 / D, None, ALU.mult,
                                             ALU.bypass)
                        vector.drain()
                        vector.tensor_tensor(var_t[:, 0:w], m_bc[:, 0:w],
                                             m_bc[:, 0:w], ALU.mult)
                        vector.drain()
                        vector.scalar_tensor_tensor(
                            var_t[:, 0:w], p_misc[:, 512:512 + w], 1.0 / D,
                            var_t[:, 0:w], ALU.mult, ALU.subtract
                            ).then_inc(svar, 1)
                    else:
                        vector.memset(m_bc[:, 0:w], 0.0)
                        vector.memset(var_t[:, 0:w], 1.0)
                        vector.drain()
                        vector.engine_nop().then_inc(svar, 1)
                    for c in ([] if "xq" in SKIP else range(KC)):
                        vector.tensor_tensor(
                            z_g[:, c * GW:c * GW + w],
                            x_g[:, xb + c * GW:xb + c * GW + w],
                            m_bc[:, 0:w], ALU.subtract)
                    vector.drain()
                    if "amax" not in SKIP:
                        vector.tensor_reduce(
                            rm[:, 0:w],
                            sap(z_g, [[KC * GW, 128], [1, w], [GW, KC]]),
                            AXL.X, ALU.max, apply_absolute_value=True
                            ).then_inc(srm, 1)
                    if "amax" not in SKIP and "amx2" not in SKIP:
                        vector.wait_ge(stp, g + 1)
                        for t in range(nt):
                            tw = min(128, w - t * 128)
                            last = vector.tensor_reduce(
                                amx[0:tw, t:t + 1],
                                p_misc[0:tw, t * 128:t * 128 + 128],
                                AXL.X, ALU.max)
                        last.then_inc(sam, 1)
                        vector.wait_ge(st2, g + 1)
                        if g >= 1 and "amx3" not in SKIP:
                            vector.wait_ge(srwh, 16 * g)
                        vector.tensor_copy(row4[0:4, :], p_misc[0:4, 384:512]
                                           ).then_inc(sr3, 1)
                    if ("amax" not in SKIP and "amx2" not in SKIP
                            and "amx3" not in SKIP):
                        vector.wait_ge(srw, 16 * (g + 1))
                        vector.tensor_copy(hlr[0:1, 0:w], arow[0:1, 0:w])
                        vector.drain()
                        vector.scalar_tensor_tensor(
                            hlr2[0:1, 0:w], arow[0:1, 0:w], 0.0,
                            hlr[0:1, 0:w],
                            ALU.add, ALU.subtract).then_inc(shl, 1)
                    if ("amax" not in SKIP and "amx2" not in SKIP
                            and "amx3" not in SKIP and "amx4" not in SKIP):
                        vector.wait_ge(sbc, g + 1)
                        vector.wait_ge(srs, g + 1)
                        vector.tensor_tensor(u_sb[:, 0:w],
                                             p_misc[:, 512:512 + w],
                                             rstd[:, 0:w], ALU.mult)
                        vector.drain()
                        vector.tensor_scalar(u_sb[:, 0:w], u_sb[:, 0:w], 1e-5,
                                             None, ALU.max, ALU.bypass)
                        vector.drain()
                        vector.tensor_scalar(q1[:, 0:w], u_sb[:, 0:w],
                                             env["q1c"][:, 0:1], None,
                                             ALU.mult, ALU.bypass
                                             ).then_inc(sga, 1)
                        vector.wait_ge(sp127, 2 * g + 1)
                        vector.tensor_tensor(gam[:, 0:w], u_sb[:, 0:w],
                                             rstd[:, 0:w], ALU.mult)
                        vector.drain()
                    if ("amax" in SKIP or "amx2" in SKIP
                            or "amx3" in SKIP or "amx4" in SKIP):
                        if "seng" not in SKIP:
                            vector.wait_ge(srs, g + 1)
                        vector.memset(q1[:, 0:w], 0.05)
                        vector.memset(gam[:, 0:w], 20.0)
                        vector.drain()
                    if "xq" in SKIP:
                        last = vector.memset(xq_g[:, 0:KC * GW], 1.0)
                        last.then_inc(sxq, 1)
                    else:
                      last = None
                      for c in range(KC):
                        last = vector._custom_dve(
                            SCR, out=xq_g[:, c * GW:c * GW + w],
                            in0=z_g[:, c * GW:c * GW + w],
                            in1=gam[:, 0:w], s0=RC2, s1=RC1)
                    for col, msk in ([] if "xq" in SKIP else
                                     ((LH - 1, "eml"), (LH + TOWN, "emr"))):
                        if a0 <= col < a1:
                            vector.drain()
                            last = vector.engine_nop()
                            j = col - a0
                            for c in range(KC):
                                last = vector.tensor_scalar(
                                    xq_g[:, c * GW + j:c * GW + j + 1],
                                    xq_g[:, c * GW + j:c * GW + j + 1],
                                    env[msk][:, 0:1], None, ALU.mult,
                                    ALU.bypass)
                    if "xq" not in SKIP:
                        last.then_inc(sxq, 1)
                    c0 = g0 - a0
                    c1v = c0 + gw
                    lo = c0 if a0 < g0 else c0 + 1
                    hi = c1v if a1 > g1 else c1v - 1
                    for fc in ([] if "mm1" in SKIP else range(FCH)):
                        sl = fc % 2
                        o = sl * GW
                        vector.wait_ge(sm1p[g % 2], FCH * (g // 2) + ((fc + 2) & ~1))
                        if fc >= 2:
                            vector.wait_ge(sslp[g % 2],
                                           FCH * (g // 2) + (fc & ~1))
                        i = vector.tensor_tensor(dq_t[:, o:o + w],
                                                 p_h[:, sl * 512:sl * 512 + w],
                                                 q1[:, 0:w], ALU.mult)
                        if fc % 2 == 1:
                            i.then_inc(scvp[g % 2], 2)
                        vector.drain()
                        if "conv" in SKIP:
                            if fc % 2 == 1:
                                vector.engine_nop().then_inc(sa2p[g % 2], 2)
                            continue
                        cw0 = g_convw[:, 3 * fc + 0:3 * fc + 1]
                        cw1 = g_convw[:, 3 * fc + 1:3 * fc + 2]
                        cw2 = g_convw[:, 3 * fc + 2:3 * fc + 3]
                        vector._custom_dve(
                            AX2, out=ctp[:, o + lo:o + hi],
                            in0=dq_t[:, o + lo - 1:o + hi - 1],
                            in1=dq_t[:, o + lo + 1:o + hi + 1],
                            s0=cw0, s1=cw2)
                        vector.drain()
                        if lo > c0:
                            vector.tensor_scalar(
                                ctp[:, o + c0:o + c0 + 1],
                                dq_t[:, o + c0 + 1:o + c0 + 2],
                                cw2, None, ALU.mult, ALU.bypass)
                            vector.drain()
                            vector.scalar_tensor_tensor(
                                ct2[:, o + c0:o + c0 + 1],
                                dq_t[:, o + c0:o + c0 + 1], cw1,
                                ctp[:, o + c0:o + c0 + 1], ALU.mult, ALU.add)
                        if hi < c1v:
                            vector.tensor_scalar(
                                ctp[:, o + c1v - 1:o + c1v],
                                dq_t[:, o + c1v - 2:o + c1v - 1],
                                cw0, None, ALU.mult, ALU.bypass)
                            vector.drain()
                            vector.scalar_tensor_tensor(
                                ct2[:, o + c1v - 1:o + c1v],
                                dq_t[:, o + c1v - 1:o + c1v], cw1,
                                ctp[:, o + c1v - 1:o + c1v], ALU.mult, ALU.add)
                        lastc = vector.scalar_tensor_tensor(
                            ct2[:, o + lo:o + hi],
                            dq_t[:, o + lo:o + hi], cw1,
                            ctp[:, o + lo:o + hi], ALU.mult, ALU.add)
                        (lastc.then_inc(sa2p[g % 2], 2) if fc % 2 == 1 else None)
                    if "mm1" not in SKIP:
                        vector.wait_ge(sslp[g % 2], FCH * (g // 2 + 1))
                    if "amax" not in SKIP:
                        vector.tensor_reduce(
                            rm[:, 0:gw],
                            sap(h_g, [[FCH * GA, 128], [1, gw], [GA, FCH]]),
                            AXL.X, ALU.max, apply_absolute_value=True
                            ).then_inc(srmh, 1)
                    if "amax" not in SKIP and "amx2" not in SKIP:
                        vector.wait_ge(stph, g + 1)
                        for t in range(nth):
                            tw = min(128, gw - t * 128)
                            last = vector.tensor_reduce(
                                amx[0:tw, t:t + 1],
                                p_misc[0:tw, t * 128:t * 128 + 128],
                                AXL.X, ALU.max)
                        last.then_inc(samh, 1)
                        vector.wait_ge(st2h, g + 1)
                        vector.tensor_copy(row4[0:4, :], p_misc[0:4, 384:512]
                                           ).then_inc(sr3h, 1)
                    if ("amax" not in SKIP and "amx2" not in SKIP
                            and "amx3" not in SKIP):
                        vector.wait_ge(srwh, 16 * (g + 1))
                        vector.tensor_copy(hlr[0:1, 0:gw], arow[0:1, 0:gw])
                        vector.drain()
                        vector.scalar_tensor_tensor(
                            hlr2[0:1, 0:gw], arow[0:1, 0:gw], 0.0,
                            hlr[0:1, 0:gw],
                            ALU.add, ALU.subtract).then_inc(shlh, 1)
                    if ("amax" not in SKIP and "amx2" not in SKIP
                            and "amx3" not in SKIP and "amx4" not in SKIP):
                        vector.wait_ge(sbch, g + 1)
                        vector.tensor_scalar(u2_sb[:, 0:gw],
                                             p_misc[:, 512:512 + gw], 1e-5,
                                             None, ALU.max, ALU.bypass)
                        vector.drain()
                        vector.tensor_scalar(q2[:, 0:gw], u2_sb[:, 0:gw],
                                             env["q2c"][:, 0:1], None,
                                             ALU.mult, ALU.bypass
                                             ).then_inc(sgh, 1)
                        vector.wait_ge(sp127, 2 * g + 2)
                    if ("amax" in SKIP or "amx2" in SKIP
                            or "amx3" in SKIP or "amx4" in SKIP):
                        vector.memset(q2[:, 0:gw], 0.05)
                        vector.memset(gamh[:, 0:gw], 20.0)
                        vector.drain()
                        vector.engine_nop().then_inc(sgh, 1)
                    if "hqmm2" not in SKIP:
                        for fc in range(FCH):
                            if fc >= 2:
                                vector.wait_ge(sm2p[g % 2],
                                               FCH * (g // 2) + (fc & ~1))
                            i = vector._custom_dve(
                                SCR,
                                out=hq_t[:, (fc % 2) * GA:(fc % 2) * GA + gw],
                                in0=h_g[:, fc * GA:fc * GA + gw],
                                in1=gamh[:, 0:gw], s0=RC2, s1=RC1)
                            if fc % 2 == 1:
                                i.then_inc(shqp[g % 2], 2)
                    if g >= 1 and "stores" not in SKIP:
                        vector.wait_ge(sxasg[g - 1], 16 * KC)
                    for dc in range(KC):
                        if "hqmm2" not in SKIP:
                            if dc < KC - 1:
                                vector.wait_ge(sm2d, (KC - 1) * g + dc + 1)
                            else:
                                vector.wait_ge(sm2p[g % 2],
                                               FCH * (g // 2 + 1))
                            sl = dc
                            vector.tensor_tensor(
                                xa_t[:, sl * GA:sl * GA + gw],
                                pdc(dc, gw), q2[:, 0:gw], ALU.mult)
                            vector.drain()
                            last = vector.tensor_tensor(
                                xa_t[:, sl * GA:sl * GA + gw],
                                xa_t[:, sl * GA:sl * GA + gw],
                                x_g[:, xb + dc * GW + (g0 - a0):
                                    xb + dc * GW + (g0 - a0) + gw], ALU.add)
                        else:
                            sl = dc
                            last = vector.memset(
                                xa_t[:, sl * GA:sl * GA + gw], 0.5)
                        last.then_inc(sxa, 1)

            @blk.tensor
            def _(tensor):
                if "wload" not in SKIP:
                    for p in range(4):
                        n = len([i for i in range(KC) if i % 4 == p]) + \
                            len([j for j in range(FCH) if (KC + j) % 4 == p])
                        tensor.wait_ge(swlp[p], 16 * n)
                for g in range(NG):
                    g0, g1, a0, a1 = grange(g)
                    w = a1 - a0
                    gw = g1 - g0
                    xb = (g % 2) * KC * GW
                    nt = (w + 127) // 128
                    nth = (gw + 127) // 128
                    tensor.wait_ge(sxg[g], 16 * KC)
                    if g >= 1:
                        tensor.wait_ge(sxa, KC * g)
                    if "stats" not in SKIP:
                        for c in range(KC):
                            tensor.matmul(p_misc[:, 0:w], g_ones[:],
                                          x_g[:, xb + c * GW:xb + c * GW + w],
                                          start=(c == 0), stop=(c == KC - 1))
                        last = None
                        for c in range(KC):
                            tensor.wait_ge(ssq, KC * g + c + 1)
                            last = tensor.matmul(
                                p_misc[:, 512:512 + w], g_ones[:],
                                sq_t[:, (c % 2) * GW:(c % 2) * GW + w],
                                start=(c == 0), stop=(c == KC - 1))
                            last.then_inc(ssmm, 1)
                    if "amax" not in SKIP and "amx2" not in SKIP:
                        tensor.wait_ge(srm, g + 1)
                        for t in range(nt):
                            tw = min(128, w - t * 128)
                            last = tensor.transpose(
                                p_misc[0:tw, t * 128:t * 128 + 128],
                                rm[:, t * 128:t * 128 + tw], g_ident[:])
                        last.then_inc(stp, 1)
                        tensor.wait_ge(sam, g + 1)
                        tensor.transpose(p_misc[0:4, 384:512], amx[:, 0:4],
                                         g_ident[:]).then_inc(st2, 1)
                    if ("amax" not in SKIP and "amx2" not in SKIP
                            and "amx3" not in SKIP and "amx4" not in SKIP):
                        tensor.wait_ge(shl, g + 1)
                        tensor.matmul(p_misc[:, 512:512 + w],
                                      g_onesbf[0:1, :],
                                      hlr[0:1, 0:w], start=True, stop=False)
                        tensor.matmul(p_misc[:, 512:512 + w],
                                      g_onesbf[0:1, :],
                                      hlr2[0:1, 0:w], start=False, stop=True
                                      ).then_inc(sbc, 1)
                    tensor.wait_ge(sxq, g + 1)
                    for fc in ([] if "mm1" in SKIP else range(FCH)):
                        sl = fc % 2
                        if fc >= 2:
                            tensor.wait_ge(scvp[g % 2],
                                           FCH * (g // 2) + (fc & ~1))
                        lastm = None
                        for c in range(KC):
                            lastm = tensor.matmul(
                                p_h[:, sl * 512:sl * 512 + w],
                                w1q[:, c * HD + fc * 128:
                                    c * HD + fc * 128 + 128],
                                xq_g[:, c * GW:c * GW + w],
                                start=(c == 0), stop=(c == KC - 1))
                        (lastm.then_inc(sm1p[g % 2], 2) if fc % 2 == 1 else None)
                    if "amax" not in SKIP and "amx2" not in SKIP:
                        tensor.wait_ge(srmh, g + 1)
                        for t in range(nth):
                            tw = min(128, gw - t * 128)
                            last = tensor.transpose(
                                p_misc[0:tw, t * 128:t * 128 + 128],
                                rm[:, t * 128:t * 128 + tw], g_ident[:])
                        last.then_inc(stph, 1)
                        tensor.wait_ge(samh, g + 1)
                        tensor.transpose(p_misc[0:4, 384:512], amx[:, 0:4],
                                         g_ident[:]).then_inc(st2h, 1)
                    if ("amax" not in SKIP and "amx2" not in SKIP
                            and "amx3" not in SKIP and "amx4" not in SKIP):
                        tensor.wait_ge(shlh, g + 1)
                        tensor.matmul(p_misc[:, 512:512 + gw],
                                      g_onesbf[0:1, :],
                                      hlr[0:1, 0:gw], start=True, stop=False)
                        tensor.matmul(p_misc[:, 512:512 + gw],
                                      g_onesbf[0:1, :],
                                      hlr2[0:1, 0:gw], start=False, stop=True
                                      ).then_inc(sbch, 1)
                    if "hqmm2" in SKIP:
                        continue
                    tensor.wait_ge(sgh, g + 1)
                    tensor.wait_ge(scvp[g % 2], FCH * (g // 2 + 1))
                    for fc in range(FCH):
                        tensor.wait_ge(shqp[g % 2], FCH * (g // 2) + ((fc + 2) & ~1))
                        for dc in range(KC):
                            mmi = tensor.matmul(
                                pdc(dc, gw),
                                w2q[:, fc * D + dc * 128:
                                    fc * D + dc * 128 + 128],
                                hq_t[:, (fc % 2) * GA:(fc % 2) * GA + gw],
                                start=(fc == 0), stop=(fc == FCH - 1))
                            if fc == FCH - 1:
                                if dc == KC - 1:
                                    mmi.then_inc(sm2p[g % 2], 2)
                                else:
                                    mmi.then_inc(sm2d, 1)
                            elif dc == KC - 1 and fc % 2 == 1:
                                mmi.then_inc(sm2p[g % 2], 2)

            @blk.scalar
            def _(scalar):
                for g in range(NG):
                    g0, g1, a0, a1 = grange(g)
                    w = a1 - a0
                    gw = g1 - g0
                    if "seng" in SKIP:
                        continue
                    scalar.wait_ge(svar, g + 1)
                    if "stats" not in SKIP:
                        scalar.activation(rstd[:, 0:w], var_t[:, 0:w], ACT.Ln,
                                          bias=env["eps"][:, 0:1])
                        scalar.drain()
                        scalar.activation(rstd[:, 0:w], rstd[:, 0:w], ACT.Exp,
                                          scale=-0.5).then_inc(srs, 1)
                    else:
                        scalar.activation(rstd[:, 0:w], var_t[:, 0:w],
                                          ACT.Copy).then_inc(srs, 1)
                    if ("amax" not in SKIP and "amx2" not in SKIP
                            and "amx3" not in SKIP and "amx4" not in SKIP):
                        scalar.wait_ge(sga, g + 1)
                        scalar.activation(u_sb[:, 0:w], u_sb[:, 0:w], ACT.Ln)
                        scalar.drain()
                        scalar.activation(u_sb[:, 0:w], u_sb[:, 0:w], ACT.Exp,
                                          scale=-1.0,
                                          bias=env["ln127"][:, 0:1]
                                          ).then_inc(sp127, 1)
                    for fc in ([] if "mm1" in SKIP else range(FCH)):
                        scalar.wait_ge(sa2p[g % 2], FCH * (g // 2) + ((fc + 2) & ~1))
                        i = scalar.activation(
                            h_g[:, fc * GA:fc * GA + gw],
                            ct2[:, (fc % 2) * GW + (g0 - a0):
                                (fc % 2) * GW + (g0 - a0) + gw],
                            ACT.Silu, bias=g_convb[:, fc:fc + 1])
                        if fc % 2 == 1:
                            i.then_inc(sslp[g % 2], 2)
                    if ("amax" not in SKIP and "amx2" not in SKIP
                            and "amx3" not in SKIP and "amx4" not in SKIP):
                        scalar.wait_ge(sgh, g + 1)
                        scalar.activation(gamh[:, 0:gw], u2_sb[:, 0:gw],
                                          ACT.Ln)
                        scalar.drain()
                        scalar.activation(gamh[:, 0:gw], gamh[:, 0:gw],
                                          ACT.Exp, scale=-1.0,
                                          bias=env["ln127"][:, 0:1]
                                          ).then_inc(sp127, 1)

            @blk.gpsimd
            def _(gpsimd):
                for g in range(NG):
                    g0, g1, a0, a1 = grange(g)
                    gw = g1 - g0
                    if ("amax" not in SKIP and "amx2" not in SKIP
                            and "amx3" not in SKIP):
                        gpsimd.wait_ge(sr3, g + 1)
                        gpsimd.dma_start(out=arow[0:1, 0:512],
                                         in_=row4[0:4, :]).then_inc(srw, 16)
                        gpsimd.wait_ge(sr3h, g + 1)
                        gpsimd.dma_start(out=arow[0:1, 0:512],
                                         in_=row4[0:4, :]).then_inc(srwh, 16)
                    for dc in ([] if "stores" in SKIP else range(KC)):
                        gpsimd.wait_ge(sxa, KC * g + dc + 1)
                        if stage == "A":
                            dst = dram_out.ap()[dc, :, g0:g1]
                        else:
                            dst = dram_out.ap()[dc, :, g0 - LH:g1 - LH]
                        gpsimd.dma_start(
                            out=dst, in_=xa_t[:, dc * GA:dc * GA + gw]
                        ).then_inc(sxasg[g], 16)
                for g in ([] if "stores" in SKIP else range(NG)):
                    gpsimd.wait_ge(sxasg[g], 16 * KC)
                if debug and stage == "A":
                    for c in range(KC):
                        gpsimd.dma_start(out=ddbg["dxa"].ap()[c, :, :],
                                         in_=dram_out.ap()[c, :, :]
                                         ).then_inc(snk, 16)
                    gpsimd.wait_ge(snk, 16 * KC)


def _emit_pb1(env):
    nc = env["nc"]
    din = env["din"]
    scr = env["scr"]
    AX2 = env["ops"]["ANT_AXPY2"]
    g_ones = env["ones"]
    from contextlib import ExitStack

    G = GB1
    assert G > LH
    groups = _grp_ranges(0, TE, G)
    NG = len(groups)
    with ExitStack() as ctx:
        SB = lambda name, shape, dt=F32: ctx.enter_context(
            nc.sbuf_tensor("b1" + name, list(shape), dt))
        PS = lambda name, shape: ctx.enter_context(
            nc.psum_tensor("b1" + name, list(shape), F32))
        sem = lambda name: ctx.enter_context(nc.semaphore("b1" + name))

        SW = G + 1
        wstg = SB("wst", (128, 2 * D))
        wk_sb = SB("wk", (128, KC * D), BF16)
        wv_sb = SB("wv", (128, KC * D), BF16)
        wr_sb = SB("wr", (128, KC * D), BF16)
        wo_sb = SB("wo", (128, KC * D), BF16)
        xa_g = SB("xa", (128, 2 * KC * SW))
        xo_g = SB("xo", (128, KC * SW))
        wg_g = SB("wg", (128, KC * SW))
        ek_g = SB("ek", (128, KC * G))
        ekv_g = SB("ekv", (128, KC * G))
        A_g = SB("A", (128, KC * G))
        B_g = SB("B", (128, KC * G))
        mx_bf = [SB("mx%d" % i, (128, KC * G), BF16) for i in range(3)]
        rw_bf = SB("rw", (128, KC * G), BF16)
        ewb = SB("ewb", (128, KC * G))
        sq_t = SB("sq", (128, SW))
        wkv_t = SB("wkv", (128, G))
        winv_t = SB("wnv", (128, G))
        carA = SB("cA", (128, KC))
        carB = SB("cB", (128, KC))
        m_bc = SB("mbc", (128, SW))
        rstd = SB("rsd", (128, SW))
        m2_bc = SB("m2b", (128, SW))
        rstd2 = SB("rs2", (128, SW))
        var_t = SB("var", (128, SW))
        epsw = SB("epw", (128, 1))
        zero_pp = SB("zpp", (128, 1))
        gmk_t = SB("gmk", (128, 3 * KC))
        gom_t = SB("gom", (128, 3 * KC))
        gom0_t = SB("gm0", (128, 3 * KC))
        bmix_f = SB("bmf", (128, 6 * KC))
        bmix_bf = SB("bmb", (128, 6 * KC), BF16)
        brows = SB("brw", (1, 6 * D), BF16)
        ones_row = SB("onr", (1, G), BF16)
        xp0_row = SB("xpr", (1, G), BF16)

        p_st = PS("pst", (128, 1024))
        p_kv = PS("pkv", (128, 1024))
        p_o = PS("po", (128, 2048))

        swl = [sem("wl%d" % i) for i in range(4)]
        swc = sem("wc")
        scst = sem("cst")
        srowm = sem("rwm")
        srowc = sem("rwc")
        sxg = [sem("xg%d" % i) for i in range(NG)]
        sxsg = [sem("xs%d" % i) for i in range(NG)]
        ssq = sem("ssq"); ssmm = sem("smm"); svar = sem("sva")
        srs = sem("srs")
        ssq2 = sem("sq2"); ssm2 = sem("sm2"); svar2 = sem("sv2")
        smix = sem("smx")
        smk = sem("smk"); smv = sem("smv"); smr = sem("smr")
        sek = sem("sek")
        sekv = sem("skv")
        srg = sem("srg")
        sscn = sem("scn")
        slnb = sem("lnb")
        srw = sem("srw")
        smo = sem("smo")
        scar = sem("car")
        sx1 = sem("sx1")
        snk = sem("snk")

        def grange(g):
            c0, c1 = groups[g]
            a0 = max(c0 - 1, 0)
            return c0, c1, a0, c1

        wmats = (wk_sb, wv_sb, wr_sb, wo_sb)

        with nc.Block("b1") as blk:
            @blk.sync
            def _(sync):
                for i, wdr in enumerate((din["wk"], din["wv"], din["wr"],
                                         din["wo"])):
                    for c in range(KC):
                        j = i * KC + c
                        if j >= 2:
                            sync.wait_ge(swc, j - 1)
                        sync.dma_start(out=wstg[:, (j % 2) * D:(j % 2) * D + D],
                                       in_=wdr.ap()[c, :, :]
                                       ).then_inc(swl[j % 4], 16)
                for g in range(NG):
                    c0, c1, a0, a1 = grange(g)
                    if g >= 2:
                        sync.wait_ge(sx1, KC * (g - 1))
                    for c in range(KC):
                        sync.dma_start(
                            out=xa_g[:, ((g % 2) * KC + c) * SW:
                                    ((g % 2) * KC + c) * SW + (a1 - a0)],
                            in_=scr["xa"].ap()[c, :, a0:a1]
                            ).then_inc(sxg[g], 16)

            @blk.vector
            def _(vector):
                vector.memset(epsw[:], 1e-8)
                vector.memset(zero_pp[:], 0.0)
                vector.memset(carA[:], 0.0)
                vector.memset(carB[:], 0.0)
                vector.memset(ones_row[0:1, :], 1.0)
                vector.memset(xp0_row[0:1, :], 1.0)
                vector.drain()
                vector.tensor_scalar(xp0_row[0:1, LH:LH + 1],
                                     xp0_row[0:1, LH:LH + 1],
                                     env["xp0"][0:1, 0:1], None,
                                     ALU.mult, ALU.bypass)
                for c in range(KC):
                    vector.memset(ewb[:, c * G:(c + 1) * G], 1.0)
                vector.drain()
                for c in range(KC):
                    vector.tensor_scalar(ewb[:, c * G:(c + 1) * G],
                                         ewb[:, c * G:(c + 1) * G],
                                         env["ew"][:, c:c + 1], None,
                                         ALU.mult, ALU.bypass)
                for mi, mx in enumerate(("mixk", "mixv", "mixr")):
                    om = ("omk", "omv", "omr")[mi]
                    vector.tensor_tensor(gmk_t[:, mi * KC:(mi + 1) * KC],
                                         env["ln1g"][:], env[mx][:], ALU.mult)
                    vector.tensor_tensor(gom_t[:, mi * KC:(mi + 1) * KC],
                                         env["ln1g"][:], env[om][:], ALU.mult)
                    vector.tensor_tensor(bmix_f[:, mi * KC:(mi + 1) * KC],
                                         env["ln1b"][:], env[mx][:], ALU.mult)
                    vector.tensor_tensor(
                        bmix_f[:, (3 + mi) * KC:(4 + mi) * KC],
                        env["ln1b"][:], env[om][:], ALU.mult)
                vector.drain()
                for mi in range(3):
                    vector.tensor_scalar(gom0_t[:, mi * KC:(mi + 1) * KC],
                                         gom_t[:, mi * KC:(mi + 1) * KC],
                                         env["xp0"][:, 0:1], None,
                                         ALU.mult, ALU.bypass)
                vector.tensor_copy(bmix_bf[:, :], bmix_f[:, :]
                                   ).then_inc(scst, 1)
                for j in range(4 * KC):
                    dst = wmats[j // KC]
                    c = j % KC
                    vector.wait_ge(swl[j % 4], 16 * (j // 4 + 1))
                    vector.tensor_copy(dst[:, c * D:(c + 1) * D],
                                       wstg[:, (j % 2) * D:(j % 2) * D + D]
                                       ).then_inc(swc, 1)
                rord = [(var * 3 + mi) * 2 + half for mi in range(3)
                        for var in range(2) for half in range(2)]
                for i, ridx in enumerate(rord):
                    vector.wait_ge(srowm, i + 1)
                    vector.tensor_copy(
                        brows[0:1, ridx * 512:(ridx + 1) * 512],
                        p_st[0:1, 0:512]).then_inc(srowc, 1)
                for g in range(NG):
                    c0, c1, a0, a1 = grange(g)
                    sw = a1 - a0
                    cw = c1 - c0
                    xb = (g % 2) * KC * SW
                    off = c0 - a0
                    vector.wait_ge(sxg[g], 16 * KC)
                    for c in range(KC):
                        if c >= 1:
                            vector.wait_ge(ssmm, KC * g + c)
                        vector.tensor_tensor(
                            sq_t[:, 0:sw],
                            xa_g[:, xb + c * SW:xb + c * SW + sw],
                            xa_g[:, xb + c * SW:xb + c * SW + sw],
                            ALU.mult).then_inc(ssq, 1)
                    if g >= 1:
                        vector.wait_ge(srs, 2 * g)
                    vector.wait_ge(ssmm, KC * (g + 1))
                    vector.tensor_scalar(m_bc[:, 0:sw], p_st[:, 0:sw],
                                         1.0 / D, None, ALU.mult, ALU.bypass)
                    vector.drain()
                    vector.tensor_tensor(var_t[:, 0:sw], m_bc[:, 0:sw],
                                         m_bc[:, 0:sw], ALU.mult)
                    vector.drain()
                    vector.scalar_tensor_tensor(
                        var_t[:, 0:sw], p_st[:, 512:512 + sw], 1.0 / D,
                        var_t[:, 0:sw], ALU.mult, ALU.subtract
                        ).then_inc(svar, 1)
                    vector.wait_ge(srs, 2 * g + 1)
                    if g >= 1:
                        vector.wait_ge(sxsg[g - 1], 16 * KC)
                    for c in range(KC):
                        vector.tensor_tensor(
                            xo_g[:, c * SW:c * SW + sw],
                            xa_g[:, xb + c * SW:xb + c * SW + sw],
                            m_bc[:, 0:sw], ALU.subtract)
                    vector.drain()
                    for c in range(KC):
                        vector.tensor_tensor(
                            xo_g[:, c * SW:c * SW + sw],
                            xo_g[:, c * SW:c * SW + sw],
                            rstd[:, 0:sw], ALU.mult)
                    vector.drain()
                    for c in range(KC):
                        if c >= 1:
                            vector.wait_ge(ssm2, KC * g + c)
                        vector.tensor_tensor(
                            sq_t[:, 0:sw],
                            xo_g[:, c * SW:c * SW + sw],
                            xo_g[:, c * SW:c * SW + sw],
                            ALU.mult).then_inc(ssq2, 1)
                    vector.wait_ge(ssm2, KC * (g + 1))
                    vector.tensor_scalar(m2_bc[:, 0:sw], p_st[:, 0:sw],
                                         1.0 / D, None, ALU.mult, ALU.bypass)
                    vector.drain()
                    vector.tensor_tensor(var_t[:, 0:sw], m2_bc[:, 0:sw],
                                         m2_bc[:, 0:sw], ALU.mult)
                    vector.drain()
                    vector.scalar_tensor_tensor(
                        var_t[:, 0:sw], p_st[:, 512:512 + sw], 1.0 / D,
                        var_t[:, 0:sw], ALU.mult, ALU.subtract
                        ).then_inc(svar2, 1)
                    vector.wait_ge(srs, 2 * g + 2)
                    for c in range(KC):
                        vector.tensor_tensor(
                            wg_g[:, c * SW:c * SW + sw],
                            xo_g[:, c * SW:c * SW + sw],
                            m2_bc[:, 0:sw], ALU.subtract)
                    vector.drain()
                    for c in range(KC):
                        vector.tensor_tensor(
                            wg_g[:, c * SW:c * SW + sw],
                            wg_g[:, c * SW:c * SW + sw],
                            rstd2[:, 0:sw], ALU.mult)
                    vector.drain()
                    for mi in range(3):
                        mmsem = (smk, smv, smr)[mi]
                        if g >= 1:
                            vector.wait_ge(mmsem, KC * g)
                        for c in range(KC):
                            gmk = gmk_t[:, mi * KC + c:mi * KC + c + 1]
                            gom = gom_t[:, mi * KC + c:mi * KC + c + 1]
                            dst = mx_bf[mi]
                            if off == 0:
                                vector.tensor_scalar(
                                    dst[:, c * G:c * G + 1],
                                    wg_g[:, c * SW:c * SW + 1],
                                    gmk, None, ALU.mult, ALU.bypass)
                                last = vector._custom_dve(
                                    AX2, out=dst[:, c * G + 1:c * G + cw],
                                    in0=wg_g[:, c * SW + 1:c * SW + cw],
                                    in1=wg_g[:, c * SW:c * SW + cw - 1],
                                    s0=gmk, s1=gom)
                            else:
                                last = vector._custom_dve(
                                    AX2, out=dst[:, c * G:c * G + cw],
                                    in0=wg_g[:, c * SW + 1:c * SW + 1 + cw],
                                    in1=wg_g[:, c * SW:c * SW + cw],
                                    s0=gmk, s1=gom)
                            if c0 <= LH < c1:
                                j = LH - c0
                                gom0 = gom0_t[:, mi * KC + c:mi * KC + c + 1]
                                vector.drain()
                                last = vector._custom_dve(
                                    AX2, out=dst[:, c * G + j:c * G + j + 1],
                                    in0=wg_g[:, c * SW + off + j:
                                             c * SW + off + j + 1],
                                    in1=wg_g[:, c * SW + off + j - 1:
                                             c * SW + off + j],
                                    s0=gmk, s1=gom0)
                            last.then_inc(smix, 1)
                    for oc in range(KC):
                        vector.wait_ge(sek, KC * g + oc + 1)
                        vector.wait_ge(smv, KC * g + oc + 1)
                        vector.tensor_tensor(
                            ekv_g[:, oc * G:oc * G + cw],
                            ek_g[:, oc * G:oc * G + cw],
                            p_kv[:, 512:512 + cw],
                            ALU.mult).then_inc(sekv, 1)
                    vector.drain()
                    for oc in range(KC):
                        if g >= 1:
                            vector.wait_ge(srw, KC * (g - 1) + oc + 1)
                        vector.tensor_tensor_scan(
                            A_g[:, oc * G:oc * G + cw],
                            ewb[:, oc * G:oc * G + cw],
                            ekv_g[:, oc * G:oc * G + cw],
                            carA[:, oc:oc + 1], ALU.mult, ALU.add)
                        vector.tensor_tensor_scan(
                            B_g[:, oc * G:oc * G + cw],
                            ewb[:, oc * G:oc * G + cw],
                            ek_g[:, oc * G:oc * G + cw],
                            carB[:, oc:oc + 1], ALU.mult, ALU.add
                            ).then_inc(sscn, 1)
                    vector.drain()
                    vector.tensor_copy(carA[:, 0:KC],
                                       sap(A_g, [[KC * G, 128], [G, KC]],
                                           offset=cw - 1))
                    vector.tensor_copy(carB[:, 0:KC],
                                       sap(B_g, [[KC * G, 128], [G, KC]],
                                           offset=cw - 1)).then_inc(scar, 1)
                    vector.drain()
                    for oc in range(KC):
                        vector.wait_ge(slnb, KC * g + oc + 1)
                        vector.wait_ge(srg, KC * g + oc + 1)
                        if g >= 1:
                            vector.wait_ge(smo, KC * g)
                        vector.tensor_tensor(wkv_t[:, 0:cw],
                                             A_g[:, oc * G:oc * G + cw],
                                             ek_g[:, oc * G:oc * G + cw],
                                             ALU.mult)
                        vector.drain()
                        vector.tensor_tensor(rw_bf[:, oc * G:oc * G + cw],
                                             wkv_t[:, 0:cw],
                                             B_g[:, oc * G:oc * G + cw],
                                             ALU.mult).then_inc(srw, 1)
                        vector.drain()
                    for dc in range(KC):
                        vector.wait_ge(smo, KC * g + dc + 1)
                        vector.tensor_tensor(
                            xo_g[:, dc * SW + off:dc * SW + off + cw],
                            p_o[:, (dc % 4) * 512:(dc % 4) * 512 + cw],
                            xo_g[:, dc * SW + off:dc * SW + off + cw],
                            ALU.add).then_inc(sx1, 1)

            @blk.tensor
            def _(tensor):
                i = 0
                for mi in range(3):
                    for var in range(2):
                        for half in range(2):
                            ridx = (var * 3 + mi) * 2 + half
                            if i == 0:
                                tensor.wait_ge(scst, 1)
                            tensor.wait_ge(swc, 8 * (mi + 1))
                            if i >= 1:
                                tensor.wait_ge(srowc, i)
                            last = None
                            for kc in range(KC):
                                last = tensor.matmul(
                                    p_st[0:1, 0:512],
                                    bmix_bf[:, (var * 3 + mi) * KC + kc:
                                            (var * 3 + mi) * KC + kc + 1],
                                    wmats[mi][:, kc * D + half * 512:
                                              kc * D + half * 512 + 512],
                                    start=(kc == 0), stop=(kc == KC - 1))
                            last.then_inc(srowm, 1)
                            i += 1
                for g in range(NG):
                    c0, c1, a0, a1 = grange(g)
                    sw = a1 - a0
                    cw = c1 - c0
                    xb = (g % 2) * KC * SW
                    tensor.wait_ge(sxg[g], 16 * KC)
                    if g == 0:
                        tensor.wait_ge(srowc, 12)
                    if g >= 1:
                        tensor.wait_ge(sx1, KC * g)
                    for c in range(KC):
                        tensor.matmul(p_st[:, 0:sw], g_ones[:],
                                      xa_g[:, xb + c * SW:xb + c * SW + sw],
                                      start=(c == 0), stop=(c == KC - 1))
                    for c in range(KC):
                        tensor.wait_ge(ssq, KC * g + c + 1)
                        tensor.matmul(
                            p_st[:, 512:512 + sw], g_ones[:],
                            sq_t[:, 0:sw],
                            start=(c == 0), stop=(c == KC - 1)
                            ).then_inc(ssmm, 1)
                    for c in range(KC):
                        if c == 0:
                            tensor.wait_ge(svar, g + 1)
                            tensor.wait_ge(ssq2, KC * g + 1)
                        tensor.matmul(p_st[:, 0:sw], g_ones[:],
                                      xo_g[:, c * SW:c * SW + sw],
                                      start=(c == 0), stop=(c == KC - 1))
                    for c in range(KC):
                        tensor.wait_ge(ssq2, KC * g + c + 1)
                        tensor.matmul(
                            p_st[:, 512:512 + sw], g_ones[:],
                            sq_t[:, 0:sw],
                            start=(c == 0), stop=(c == KC - 1)
                            ).then_inc(ssm2, 1)
                    for mi, msem in enumerate((smk, smv, smr)):
                        bank = (0, 512, 0)[mi]
                        for oc in range(KC):
                            if mi == 0:
                                if oc == 0 and g >= 1:
                                    tensor.wait_ge(srg, KC * g)
                                if oc >= 1:
                                    tensor.wait_ge(sek, KC * g + oc)
                            if mi == 1:
                                tensor.wait_ge(sekv, KC * g + oc)
                            if mi == 2:
                                if oc == 0:
                                    tensor.wait_ge(sek, KC * (g + 1))
                                else:
                                    tensor.wait_ge(srg, KC * g + oc)
                            for kc in range(KC):
                                tensor.wait_ge(smix,
                                               3 * KC * g + mi * KC + kc + 1)
                                tensor.matmul(
                                    p_kv[:, bank:bank + cw],
                                    wmats[mi][:, kc * D + oc * 128:
                                              kc * D + oc * 128 + 128],
                                    mx_bf[mi][:, kc * G:kc * G + cw],
                                    start=(kc == 0), stop=False)
                            rb = (0 * 3 + mi) * 2
                            ro = (1 * 3 + mi) * 2
                            tensor.matmul(
                                p_kv[:, bank:bank + cw],
                                brows[0:1, rb * 512 + oc * 128:
                                      rb * 512 + oc * 128 + 128],
                                ones_row[0:1, 0:cw],
                                start=False, stop=False)
                            rhs2 = xp0_row if g == 0 else ones_row
                            tensor.matmul(
                                p_kv[:, bank:bank + cw],
                                brows[0:1, ro * 512 + oc * 128:
                                      ro * 512 + oc * 128 + 128],
                                rhs2[0:1, 0:cw],
                                start=False, stop=True).then_inc(msem, 1)
                    if g == 0:
                        tensor.wait_ge(swc, 32)
                    for half in range(2):
                        for dc in range(4 * half, 4 * half + 4):
                            last = None
                            for oc in range(KC):
                                tensor.wait_ge(srw, KC * g + oc + 1)
                                last = tensor.matmul(
                                    p_o[:, (dc % 4) * 512:(dc % 4) * 512 + cw],
                                    wo_sb[:, oc * D + dc * 128:
                                          oc * D + dc * 128 + 128],
                                    rw_bf[:, oc * G:oc * G + cw],
                                    start=(oc == 0), stop=(oc == KC - 1))
                            last.then_inc(smo, 1)
                        if half == 0:
                            tensor.wait_ge(sx1, KC * g + 4)

            @blk.scalar
            def _(scalar):
                for g in range(NG):
                    c0, c1, a0, a1 = grange(g)
                    sw = a1 - a0
                    cw = c1 - c0
                    scalar.wait_ge(svar, g + 1)
                    scalar.activation(rstd[:, 0:sw], var_t[:, 0:sw], ACT.Ln,
                                      bias=env["eps"][:, 0:1])
                    scalar.drain()
                    scalar.activation(rstd[:, 0:sw], rstd[:, 0:sw], ACT.Exp,
                                      scale=-0.5).then_inc(srs, 1)
                    scalar.wait_ge(svar2, g + 1)
                    scalar.activation(rstd2[:, 0:sw], var_t[:, 0:sw], ACT.Ln,
                                      bias=env["eps"][:, 0:1])
                    scalar.drain()
                    scalar.activation(rstd2[:, 0:sw], rstd2[:, 0:sw],
                                      ACT.Exp, scale=-0.5).then_inc(srs, 1)
                    for oc in range(KC):
                        scalar.wait_ge(smk, KC * g + oc + 1)
                        if g >= 1:
                            scalar.wait_ge(srw, KC * (g - 1) + oc + 1)
                        if c0 < LH:
                            pcut = min(LH - c0, cw)
                            scalar.activation(ek_g[:, oc * G:oc * G + pcut],
                                              p_kv[:, 0:pcut], ACT.Exp,
                                              bias=env["pen"][:, 0:1])
                            last = scalar.activation(
                                ek_g[:, oc * G + pcut:oc * G + cw],
                                p_kv[:, pcut:cw], ACT.Exp)
                        else:
                            last = scalar.activation(
                                ek_g[:, oc * G:oc * G + cw],
                                p_kv[:, 0:cw], ACT.Exp)
                        if c0 <= LH < c1:
                            j = LH - c0
                            scalar.drain()
                            last = scalar.activation(
                                ek_g[:, oc * G + j:oc * G + j + 1],
                                p_kv[:, j:j + 1], ACT.Exp,
                                bias=env["tfv"][:, oc:oc + 1])
                        last.then_inc(sek, 1)
                    for oc in range(KC):
                        scalar.wait_ge(sscn, KC * g + oc + 1)
                        scalar.activation(winv_t[:, 0:cw],
                                          B_g[:, oc * G:oc * G + cw],
                                          ACT.Ln, bias=epsw[:, 0:1])
                        scalar.drain()
                        scalar.activation(ek_g[:, oc * G:oc * G + cw],
                                          winv_t[:, 0:cw], ACT.Exp,
                                          scale=-1.0).then_inc(slnb, 1)
                        scalar.drain()
                    scalar.wait_ge(scar, g + 1)
                    for oc in range(KC):
                        scalar.wait_ge(smr, KC * g + oc + 1)
                        scalar.activation(B_g[:, oc * G:oc * G + cw],
                                          p_kv[:, 0:cw], ACT.Sigmoid
                                          ).then_inc(srg, 1)

            @blk.gpsimd
            def _(gpsimd):
                for g in range(NG):
                    c0, c1, a0, a1 = grange(g)
                    cw = c1 - c0
                    off = c0 - a0
                    for dc in range(KC):
                        gpsimd.wait_ge(sx1, KC * g + dc + 1)
                        gpsimd.dma_start(
                            out=scr["x1"].ap()[dc, :, c0:c1],
                            in_=xo_g[:, dc * SW + off:dc * SW + off + cw]
                            ).then_inc(sxsg[g], 16)
                for g in range(NG):
                    gpsimd.wait_ge(sxsg[g], 16 * KC)
                if env["debug"]:
                    for c in range(KC):
                        gpsimd.dma_start(out=env["ddbg"]["dx1"].ap()[c, :, :],
                                         in_=scr["x1"].ap()[c, :, :]
                                         ).then_inc(snk, 16)
                    gpsimd.wait_ge(snk, 16 * KC)


def _emit_pb2(env):
    nc = env["nc"]
    din = env["din"]
    scr = env["scr"]
    AX2 = env["ops"]["ANT_AXPY2"]
    g_ones = env["ones"]
    from contextlib import ExitStack

    G2 = GB2
    groups = _grp_ranges(LH - 1, LH + TOWN + 1, G2)
    NG = len(groups)
    with ExitStack() as ctx:
        SB = lambda name, shape, dt=F32: ctx.enter_context(
            nc.sbuf_tensor("b2" + name, list(shape), dt))
        PS = lambda name, shape: ctx.enter_context(
            nc.psum_tensor("b2" + name, list(shape), F32))
        sem = lambda name: ctx.enter_context(nc.semaphore("b2" + name))

        SW2 = G2 + 1
        wstg = SB("wst", (128, 2 * 1024))
        wfk_sb = SB("wfk", (128, KC * HD), BF16)
        wfv_sb = SB("wfv", (128, FCH * D), BF16)
        wfr_sb = SB("wfr", (128, KC * D), BF16)
        x1_g = SB("x1", (128, KC * SW2))
        xa_g = SB("xa", (128, KC * G2))
        core_g = SB("cor", (128, KC * SW2))
        ym_bf = SB("ym", (128, KC * G2), BF16)
        yn_bf = SB("yn", (128, KC * G2), BF16)
        kk_g = SB("kk", (128, FCH * G2), BF16)
        rr_t = SB("rr", (128, 2 * G2))
        rl_t = SB("rl", (128, 2 * G2))
        sq_t = SB("sq", (128, 2 * SW2))
        m_bc = SB("mbc", (128, SW2))
        rstd = SB("rsd", (128, SW2))
        var_t = SB("var", (128, SW2))
        hg2_t = SB("hg2", (128, 2 * KC))
        g2_t = SB("g2c", (128, KC))
        b2_bf = SB("b2b", (128, 2 * KC), BF16)
        b2_f = SB("b2f", (128, 2 * KC))
        brow_fk = SB("bfk", (1, HD), BF16)
        brow_fr = SB("bfr", (1, D), BF16)
        ones_row = SB("onr", (1, G2), BF16)
        xp0_row = SB("xpr", (1, G2), BF16)

        p_st = PS("pst", (128, 1024))
        p_kk = PS("pkk", (128, 1024))
        p_y = PS("py", (128, 2048))

        NLD = 4 * KC + FCH + KC
        swl = [sem("wl%d" % i) for i in range(6)]
        swc = sem("wc")
        scst = sem("cst")
        srowm = sem("rwm"); srowc = sem("rwc")
        sxg = [sem("xg%d" % i) for i in range(NG)]
        sxga = [sem("xa%d" % i) for i in range(NG)]
        sxsg = [sem("xs%d" % i) for i in range(NG)]
        ssq = sem("ssq"); ssmm = sem("smm"); svar = sem("sva"); srs = sem("srs")
        smix = sem("smx")
        smfkp = [sem("fka"), sem("fkb")]
        skkp = [sem("kka"), sem("kkb")]
        smm2 = sem("mm2")
        smr2 = sem("mr2")
        srr = sem("srr")
        sxb = sem("sxb")
        snk = sem("snk")

        def grange(g):
            c0, c1 = groups[g]
            return c0, c1, c0 - 1, c1

        def wsrc(j):
            if j < 4 * KC:
                return din["wfk"].ap()[j // 4, :, (j % 4) * 1024:
                                       (j % 4) * 1024 + 1024]
            if j < 4 * KC + FCH:
                return din["wfv"].ap()[j - 4 * KC, :, :]
            return din["wfr"].ap()[j - 4 * KC - FCH, :, :]

        def wdst(j):
            if j < 4 * KC:
                return wfk_sb[:, (j // 4) * HD + (j % 4) * 1024:
                              (j // 4) * HD + (j % 4) * 1024 + 1024]
            if j < 4 * KC + FCH:
                jj = j - 4 * KC
                return wfv_sb[:, jj * D:(jj + 1) * D]
            jj = j - 4 * KC - FCH
            return wfr_sb[:, jj * D:(jj + 1) * D]

        with nc.Block("b2") as blk:
            @blk.sync
            def _(sync):
                for j in range(NLD):
                    if j >= 2:
                        sync.wait_ge(swc, j - 1)
                    sync.dma_start(
                        out=wstg[:, (j % 2) * 1024:(j % 2) * 1024 + 1024],
                        in_=wsrc(j)).then_inc(swl[j % 6], 16)
                for g in range(NG):
                    c0, c1, a0, a1 = grange(g)
                    if g >= 1:
                        sync.wait_ge(sxb, KC * g)
                    for c in range(KC):
                        sync.dma_start(
                            out=x1_g[:, c * SW2:c * SW2 + (a1 - a0)],
                            in_=scr["x1"].ap()[c, :, a0:a1]
                            ).then_inc(sxg[g], 16)
                    for c in range(KC):
                        sync.dma_start(
                            out=xa_g[:, c * G2:c * G2 + (c1 - c0)],
                            in_=scr["xa"].ap()[c, :, c0:c1]
                            ).then_inc(sxga[g], 16)

            @blk.vector
            def _(vector):
                vector.memset(ones_row[0:1, :], 1.0)
                vector.memset(xp0_row[0:1, :], 1.0)
                vector.drain()
                vector.tensor_scalar(xp0_row[0:1, 1:2], xp0_row[0:1, 1:2],
                                     env["xp0"][0:1, 0:1], None,
                                     ALU.mult, ALU.bypass)
                vector.tensor_scalar(hg2_t[:, 0:KC], env["ln2g"][:],
                                     0.5, None, ALU.mult, ALU.bypass)
                vector.tensor_scalar(b2_f[:, 0:KC], env["ln2b"][:],
                                     0.5, None, ALU.mult, ALU.bypass)
                vector.tensor_copy(g2_t[:, 0:KC], env["ln2g"][:])
                vector.tensor_copy(b2_f[:, KC:2 * KC], env["ln2b"][:])
                vector.drain()
                vector.tensor_scalar(hg2_t[:, KC:2 * KC], hg2_t[:, 0:KC],
                                     env["xp0"][:, 0:1], None,
                                     ALU.mult, ALU.bypass)
                vector.tensor_copy(b2_bf[:, :], b2_f[:, :]).then_inc(scst, 1)
                for j in range(NLD):
                    vector.wait_ge(swl[j % 6], 16 * (j // 6 + 1))
                    vector.tensor_copy(
                        wdst(j),
                        wstg[:, (j % 2) * 1024:(j % 2) * 1024 + 1024]
                        ).then_inc(swc, 1)
                for i in range(10):
                    vector.wait_ge(srowm, i + 1)
                    dst = (brow_fk[0:1, i * 512:(i + 1) * 512] if i < 8
                           else brow_fr[0:1, (i - 8) * 512:(i - 7) * 512])
                    vector.tensor_copy(dst, p_st[0:1, 0:512]
                                       ).then_inc(srowc, 1)
                for g in range(NG):
                    c0, c1, a0, a1 = grange(g)
                    sw = a1 - a0
                    cw = c1 - c0
                    vector.wait_ge(sxg[g], 16 * KC)
                    for c in range(KC):
                        if c >= 2:
                            vector.wait_ge(ssmm, KC * g + c - 1)
                        vector.tensor_tensor(
                            sq_t[:, (c % 2) * SW2:(c % 2) * SW2 + sw],
                            x1_g[:, c * SW2:c * SW2 + sw],
                            x1_g[:, c * SW2:c * SW2 + sw],
                            ALU.mult).then_inc(ssq, 1)
                    if g >= 1:
                        vector.wait_ge(srs, g)
                    vector.wait_ge(ssmm, KC * (g + 1))
                    vector.tensor_scalar(m_bc[:, 0:sw], p_st[:, 0:sw],
                                         1.0 / D, None, ALU.mult, ALU.bypass)
                    vector.drain()
                    vector.tensor_tensor(var_t[:, 0:sw], m_bc[:, 0:sw],
                                         m_bc[:, 0:sw], ALU.mult)
                    vector.drain()
                    vector.scalar_tensor_tensor(
                        var_t[:, 0:sw], p_st[:, 512:512 + sw], 1.0 / D,
                        var_t[:, 0:sw], ALU.mult, ALU.subtract
                        ).then_inc(svar, 1)
                    vector.wait_ge(sxga[g], 16 * KC)
                    for c in range(KC):
                        vector.tensor_tensor(
                            xa_g[:, c * G2:c * G2 + cw],
                            x1_g[:, c * SW2 + 1:c * SW2 + 1 + cw],
                            xa_g[:, c * G2:c * G2 + cw], ALU.add)
                    vector.drain()
                    vector.wait_ge(srs, g + 1)
                    if g >= 1:
                        vector.wait_ge(sxsg[g - 1], 16 * KC)
                    for c in range(KC):
                        vector.tensor_tensor(
                            core_g[:, c * SW2:c * SW2 + sw],
                            x1_g[:, c * SW2:c * SW2 + sw],
                            m_bc[:, 0:sw], ALU.subtract)
                    vector.drain()
                    for c in range(KC):
                        vector.tensor_tensor(
                            core_g[:, c * SW2:c * SW2 + sw],
                            core_g[:, c * SW2:c * SW2 + sw],
                            rstd[:, 0:sw], ALU.mult)
                    vector.drain()
                    if g >= 1:
                        vector.wait_ge(smfkp[(g - 1) % 2],
                                       FCH * ((g - 1) // 2 + 1))
                    for c in range(KC):
                        hg2 = hg2_t[:, c:c + 1]
                        last = vector._custom_dve(
                            AX2, out=ym_bf[:, c * G2:c * G2 + cw],
                            in0=core_g[:, c * SW2 + 1:c * SW2 + 1 + cw],
                            in1=core_g[:, c * SW2:c * SW2 + cw],
                            s0=hg2, s1=hg2)
                        if g == 0:
                            j = LH - c0
                            vector.drain()
                            last = vector._custom_dve(
                                AX2, out=ym_bf[:, c * G2 + j:c * G2 + j + 1],
                                in0=core_g[:, c * SW2 + 1 + j:
                                           c * SW2 + 2 + j],
                                in1=core_g[:, c * SW2 + j:c * SW2 + 1 + j],
                                s0=hg2, s1=hg2_t[:, KC + c:KC + c + 1])
                        last.then_inc(smix, 1)
                    if g >= 1:
                        vector.wait_ge(smr2, KC * g)
                    for c in range(KC):
                        vector.tensor_scalar(
                            yn_bf[:, c * G2:c * G2 + cw],
                            core_g[:, c * SW2 + 1:c * SW2 + 1 + cw],
                            g2_t[:, c:c + 1], None, ALU.mult, ALU.bypass
                            ).then_inc(smix, 1)
                    for dc in range(KC):
                        vector.wait_ge(smm2, KC * g + dc + 1)
                        vector.wait_ge(srr, KC * g + dc + 1)
                        vector.tensor_tensor(
                            core_g[:, dc * SW2 + 1:dc * SW2 + 1 + cw],
                            p_y[:, (dc % 4) * 512:(dc % 4) * 512 + cw],
                            rr_t[:, (dc % 2) * G2:(dc % 2) * G2 + cw],
                            ALU.mult)
                        vector.drain()
                        vector.tensor_tensor(
                            core_g[:, dc * SW2 + 1:dc * SW2 + 1 + cw],
                            core_g[:, dc * SW2 + 1:dc * SW2 + 1 + cw],
                            xa_g[:, dc * G2:dc * G2 + cw],
                            ALU.add).then_inc(sxb, 1)

            @blk.tensor
            def _(tensor):
                for i in range(10):
                    if i == 0:
                        tensor.wait_ge(scst, 1)
                        tensor.wait_ge(swc, 4 * KC)
                    if i == 8:
                        tensor.wait_ge(swc, NLD)
                    if i >= 1:
                        tensor.wait_ge(srowc, i)
                    last = None
                    for kc in range(KC):
                        src = (wfk_sb[:, kc * HD + i * 512:
                                      kc * HD + i * 512 + 512] if i < 8
                               else wfr_sb[:, kc * D + (i - 8) * 512:
                                           kc * D + (i - 8) * 512 + 512])
                        col = 0 if i < 8 else KC
                        last = tensor.matmul(p_st[0:1, 0:512],
                                             b2_bf[:, col + kc:col + kc + 1],
                                             src, start=(kc == 0),
                                             stop=(kc == KC - 1))
                    last.then_inc(srowm, 1)
                for g in range(NG):
                    c0, c1, a0, a1 = grange(g)
                    sw = a1 - a0
                    cw = c1 - c0
                    tensor.wait_ge(sxg[g], 16 * KC)
                    if g == 0:
                        tensor.wait_ge(srowc, 10)
                    if g >= 1:
                        tensor.wait_ge(sxb, KC * g)
                        tensor.wait_ge(srr, KC * g)
                    for c in range(KC):
                        tensor.matmul(p_st[:, 0:sw], g_ones[:],
                                      x1_g[:, c * SW2:c * SW2 + sw],
                                      start=(c == 0), stop=(c == KC - 1))
                    for c in range(KC):
                        tensor.wait_ge(ssq, KC * g + c + 1)
                        tensor.matmul(
                            p_st[:, 512:512 + sw], g_ones[:],
                            sq_t[:, (c % 2) * SW2:(c % 2) * SW2 + sw],
                            start=(c == 0), stop=(c == KC - 1)
                            ).then_inc(ssmm, 1)
                    for fc in range(FCH):
                        if fc >= 2:
                            tensor.wait_ge(skkp[g % 2],
                                           FCH * (g // 2) + fc - 1)
                        for kc in range(KC):
                            tensor.wait_ge(smix, 16 * g + kc + 1)
                            tensor.matmul(
                                p_kk[:, (fc % 2) * 512:(fc % 2) * 512 + cw],
                                wfk_sb[:, kc * HD + fc * 128:
                                       kc * HD + fc * 128 + 128],
                                ym_bf[:, kc * G2:kc * G2 + cw],
                                start=(kc == 0), stop=False)
                        tensor.matmul(
                            p_kk[:, (fc % 2) * 512:(fc % 2) * 512 + cw],
                            brow_fk[0:1, fc * 128:fc * 128 + 128],
                            ones_row[0:1, 0:cw], start=False, stop=False)
                        rhs2 = xp0_row if g == 0 else ones_row
                        tensor.matmul(
                            p_kk[:, (fc % 2) * 512:(fc % 2) * 512 + cw],
                            brow_fk[0:1, fc * 128:fc * 128 + 128],
                            rhs2[0:1, 0:cw], start=False, stop=True
                            ).then_inc(smfkp[g % 2], 1)
                    for half in range(2):
                        for dc in range(4 * half, 4 * half + 4):
                            last = None
                            for fc in range(FCH):
                                tensor.wait_ge(skkp[g % 2],
                                               FCH * (g // 2) + fc + 1)
                                last = tensor.matmul(
                                    p_y[:, (dc % 4) * 512:
                                        (dc % 4) * 512 + cw],
                                    wfv_sb[:, fc * D + dc * 128:
                                           fc * D + dc * 128 + 128],
                                    kk_g[:, fc * G2:fc * G2 + cw],
                                    start=(fc == 0), stop=(fc == FCH - 1))
                            last.then_inc(smm2, 1)
                        for dc in range(4 * half, 4 * half + 4):
                            if dc == 0:
                                tensor.wait_ge(svar, g + 1)
                            if dc >= 2:
                                tensor.wait_ge(srr, KC * g + dc - 1)
                            for kc in range(KC):
                                tensor.wait_ge(smix, 16 * g + 8 + kc + 1)
                                tensor.matmul(
                                    p_st[:, (dc % 2) * 512:
                                         (dc % 2) * 512 + cw],
                                    wfr_sb[:, kc * D + dc * 128:
                                           kc * D + dc * 128 + 128],
                                    yn_bf[:, kc * G2:kc * G2 + cw],
                                    start=(kc == 0), stop=False)
                            tensor.matmul(
                                p_st[:, (dc % 2) * 512:(dc % 2) * 512 + cw],
                                brow_fr[0:1, dc * 128:dc * 128 + 128],
                                ones_row[0:1, 0:cw], start=False, stop=True
                                ).then_inc(smr2, 1)
                        if half == 0:
                            tensor.wait_ge(sxb, KC * g + 4)

            @blk.scalar
            def _(scalar):
                for g in range(NG):
                    c0, c1, a0, a1 = grange(g)
                    sw = a1 - a0
                    cw = c1 - c0
                    scalar.wait_ge(svar, g + 1)
                    scalar.activation(rstd[:, 0:sw], var_t[:, 0:sw], ACT.Ln,
                                      bias=env["eps"][:, 0:1])
                    scalar.drain()
                    scalar.activation(rstd[:, 0:sw], rstd[:, 0:sw], ACT.Exp,
                                      scale=-0.5).then_inc(srs, 1)
                    for fc in range(FCH):
                        scalar.wait_ge(smfkp[g % 2],
                                       FCH * (g // 2) + fc + 1)
                        if g >= 1 and fc == 0:
                            scalar.wait_ge(smm2, KC * g)
                        scalar.activation(
                            rl_t[:, (fc % 2) * G2:(fc % 2) * G2 + cw],
                            p_kk[:, (fc % 2) * 512:(fc % 2) * 512 + cw],
                            ACT.Relu)
                        scalar.drain()
                        scalar.activation(
                            kk_g[:, fc * G2:fc * G2 + cw],
                            rl_t[:, (fc % 2) * G2:(fc % 2) * G2 + cw],
                            ACT.Square).then_inc(skkp[g % 2], 1)
                        scalar.drain()
                    for dc in range(KC):
                        scalar.wait_ge(smr2, KC * g + dc + 1)
                        if g >= 1 or dc >= 2:
                            scalar.wait_ge(sxb, KC * g + dc - 1)
                        scalar.activation(
                            rr_t[:, (dc % 2) * G2:(dc % 2) * G2 + cw],
                            p_st[:, (dc % 2) * 512:(dc % 2) * 512 + cw],
                            ACT.Sigmoid).then_inc(srr, 1)

            @blk.gpsimd
            def _(gpsimd):
                for g in range(NG):
                    c0, c1, a0, a1 = grange(g)
                    cw = c1 - c0
                    for dc in range(KC):
                        gpsimd.wait_ge(sxb, KC * g + dc + 1)
                        gpsimd.dma_start(
                            out=scr["xb"].ap()[dc, :, c0:c1],
                            in_=core_g[:, dc * SW2 + 1:dc * SW2 + 1 + cw]
                            ).then_inc(sxsg[g], 16)
                for g in range(NG):
                    gpsimd.wait_ge(sxsg[g], 16 * KC)
                if env["debug"]:
                    for c in range(KC):
                        gpsimd.dma_start(
                            out=env["ddbg"]["dxb"].ap()[c, :,
                                                        LH - 1:LH + TOWN + 1],
                            in_=scr["xb"].ap()[c, :, LH - 1:LH + TOWN + 1]
                            ).then_inc(snk, 16)
                    gpsimd.wait_ge(snk, 16 * KC)


def _emit_tail(env, phases):
    if "PC" in phases:
        return
    nc = env["nc"]
    with nc.semaphore("tlv") as sv, nc.semaphore("tl") as st, \
            nc.sbuf_tensor("tl0", [128, TOWN], F32) as zt, \
            nc.Block("tail") as blk:
        @blk.vector
        def _(vector):
            vector.memset(zt[:], 0.0).then_inc(sv, 1)

        @blk.gpsimd
        def _(gpsimd):
            gpsimd.wait_ge(sv, 1)
            for c in range(KC):
                gpsimd.dma_start(out=env["d_out"].ap()[c, :, :], in_=zt[:]
                                 ).then_inc(st, 16)
            gpsimd.wait_ge(st, 16 * KC)


def _host_inputs(inputs, core):
    x = np.asarray(inputs["L0_attn_x"])[0]
    T = x.shape[0]
    s = core * TOWN
    lo, hi = s - LH, s + TOWN + RH
    pad_l, pad_r = max(0, -lo), max(0, hi - T)
    seg = x[max(lo, 0):min(hi, T)]
    if pad_l or pad_r:
        seg = np.pad(seg, ((pad_l, pad_r), (0, 0)))
    xfm = np.ascontiguousarray(seg.T.reshape(KC, 128, TE))

    def chunkT(w):
        w = np.ascontiguousarray(w)
        return np.ascontiguousarray(w.reshape(-1, 128, w.shape[1]))

    def vec8(v):
        return np.ascontiguousarray(np.asarray(v).reshape(KC, 128).T)

    z1 = np.zeros((128, 1), np.float32)
    o1 = np.ones((128, 1), np.float32)
    return {
        "xfm": xfm,
        "w1t": chunkT(np.asarray(inputs["L0_ffn_w1"]).T),
        "w2t": chunkT(np.asarray(inputs["L0_ffn_w2"]).T),
        "wk": chunkT(np.asarray(inputs["L0_attn_Wk"])),
        "wv": chunkT(np.asarray(inputs["L0_attn_Wv"])),
        "wr": chunkT(np.asarray(inputs["L0_attn_Wr"])),
        "wo": chunkT(np.asarray(inputs["L0_attn_Wo"])),
        "wfr": chunkT(np.asarray(inputs["L0_attn_Wfr"])),
        "wfk": chunkT(np.asarray(inputs["L0_attn_Wfk"])),
        "wfv": chunkT(np.asarray(inputs["L0_attn_Wfv"])),
        "convw": np.ascontiguousarray(
            np.asarray(inputs["L0_ffn_conv_w"])[:, 0, :]
            .reshape(FCH, 128, 3).transpose(1, 0, 2).reshape(128, 3 * FCH)),
        "convb": np.ascontiguousarray(
            np.asarray(inputs["L0_ffn_conv_b"]).reshape(FCH, 128).T),
        "ln1g": vec8(inputs["L0_attn_ln1_g"]),
        "ln1b": vec8(inputs["L0_attn_ln1_b"]),
        "td": vec8(inputs["L0_attn_time_decay"]),
        "mixk": vec8(inputs["L0_attn_mix_k"]),
        "mixv": vec8(inputs["L0_attn_mix_v"]),
        "mixr": vec8(inputs["L0_attn_mix_r"]),
        "ln2g": vec8(inputs["L0_attn_ln2_g"]),
        "ln2b": vec8(inputs["L0_attn_ln2_b"]),
        "tfv": (vec8(inputs["L0_attn_time_first"]) if core == 0
                else np.zeros((128, KC), np.float32)),
        "pen": (np.full((128, 1), -1000.0, np.float32) if core == 0 else z1),
        "xp0": (z1 if core == 0 else o1),
        "eml": (z1 if core == 0 else o1),
        "emr": (z1 if core == NCORE - 1 else o1),
        "ident": np.eye(128, dtype=np.float32),
    }


def kernel(**inputs):
    nc = build(debug=False)
    lower_extended_insts(nc)
    in_maps = [_host_inputs(inputs, c) for c in range(NCORE)]
    res = run_bass_kernel_spmd(nc, in_maps, core_ids=list(range(NCORE)))
    T = np.asarray(inputs["L0_attn_x"]).shape[1]
    out = np.empty((1, T, D), np.float32)
    for c in range(NCORE):
        o = np.asarray(res.results[c]["out"])
        out[0, c * TOWN:(c + 1) * TOWN, :] = o.reshape(D, TOWN).T
    return out


# version 05 (final)
# speedup vs baseline: 1.0447x; 1.0447x over previous
import numpy as np
import concourse.bass as bass
import concourse.mybir as mybir
from concourse.bass_utils import run_bass_kernel_spmd
from concourse.library_overlay import lower_extended_insts
import concourse.dve_ops as dve_ops
import concourse.dve_spec as dve_spec
from concourse.dve_op import DveOp
from concourse.dve_spec import Spec, Src0, Src1, C0, C1

F32 = mybir.dt.float32
BF16 = mybir.dt.bfloat16
ALU = mybir.AluOpType
ACT = mybir.ActivationFunctionType
AXL = mybir.AxisListType

NCORE = 8
D = 1024
HD = 4096
KC = 8
FCH = 32
TOWN = 2048
LH = 160
RH = 2
TE = LH + TOWN + RH
RC1 = float(1.5 * 2.0 ** 23)
RC2 = float(1.5 * 2.0 ** 52)
GA = 224
GB1 = 288
GB2 = 184
PA_MAXG = None
SKIP = set()
EPS_LN = 1e-5
LN127 = float(np.log(127.0))

_OPS = {}


def _register_ops():
    if _OPS:
        return _OPS

    from concourse.dve_ops_raw import _compute_dtype

    def mk(name, spec, reference):
        for op in dve_ops.OPS:
            if op.name == name:
                _OPS[name] = op
                return op
        shas = {}
        for ver in ("v3", "v4"):
            r = dve_spec.lower(spec, ver=ver, perf_en=False, name=name)
            shas[ver] = r.sha(ver)
        op = DveOp(name, spec, subdim=False, uops_sha=shas,
                   reference=reference)
        victim = next(i for i, o in enumerate(dve_ops.OPS)
                      if o.name.startswith("DVEHASH32L"))
        vrow = dve_ops._SUB_OPCODE_FOR_NAME[dve_ops.OPS[victim].name]
        del dve_ops._SUB_OPCODE_FOR_NAME[dve_ops.OPS[victim].name]
        dve_ops.OPS[victim] = op
        dve_ops._SUB_OPCODE_FOR_NAME[name] = vrow
        dve_ops.CUSTOM_DVE_REFS[name] = op.reference
        dve_ops.CUSTOM_DVE_SPECS[name] = spec
        _OPS[name] = op
        return op

    _m = Src0 * Src1
    _u = (_m + C0) - C0
    _w = ((_m - _u) + C1) - C1
    mk("ANT_SCALE_ROUND", Spec(body=_u + _w),
       lambda in0, in1, s0, s1, imm2: np.round(
           in0.astype(_compute_dtype(in0)) * in1))
    mk("ANT_AXPY2", Spec(body=Src0 * C0 + Src1 * C1),
       lambda in0, in1, s0, s1, imm2:
           in0.astype(_compute_dtype(in0)) * s0 + in1 * s1)
    return _OPS


def _grp_ranges(start, end, step):
    return [(a, min(a + step, end)) for a in range(start, end, step)]


def sap(h, pattern, offset=0):
    return bass.AP(h, offset, pattern)


def build(debug=False, phases=("P0", "PA", "PB1", "PB2", "PC")):
    ops = _register_ops()
    nc = bass.Bass("TRN2", target_bir_lowering=False)

    din = {}

    def dri(name, shape, dt=F32):
        din[name] = nc.dram_tensor(name, list(shape), dt, kind="ExternalInput")

    dri("xfm", (KC, 128, TE))
    dri("w1t", (KC, 128, HD))
    dri("w2t", (FCH, 128, D))
    for nm in ("wk", "wv", "wr", "wo", "wfr"):
        dri(nm, (KC, 128, D))
    dri("wfk", (KC, 128, HD))
    dri("wfv", (FCH, 128, D))
    dri("convw", (128, 3 * FCH))
    dri("convb", (128, FCH))
    for nm in ("ln1g", "ln1b", "td", "mixk", "mixv", "mixr", "ln2g",
               "ln2b", "tfv"):
        dri(nm, (128, KC))
    for nm in ("pen", "xp0", "eml", "emr"):
        dri(nm, (128, 1))
    dri("ident", (128, 128))

    d_out = nc.dram_tensor("out", [KC, 128, TOWN], F32, kind="ExternalOutput")
    ddbg = {}
    if debug:
        for nm in ("dxa", "dx1", "dxb"):
            ddbg[nm] = nc.dram_tensor(nm, [KC, 128, TE], F32,
                                      kind="ExternalOutput")
        ddbg["dsc"] = nc.dram_tensor("dsc", [128, 16], F32,
                                     kind="ExternalOutput")

    scr = {
        "xa": nc.dram_tensor("s_xa", [KC, 128, TE], F32, kind="Internal"),
        "x1": nc.dram_tensor("s_x1", [KC, 128, TE], F32, kind="Internal"),
        "xb": nc.dram_tensor("s_xb", [KC, 128, TE], F32, kind="Internal"),
        "w1q": nc.dram_tensor("s_w1q", [KC, 128, HD], BF16, kind="Internal"),
        "w2q": nc.dram_tensor("s_w2q", [FCH, 128, D], BF16, kind="Internal"),
    }

    from contextlib import ExitStack
    with ExitStack() as glob:
        SB = lambda name, shape, dt=F32: glob.enter_context(
            nc.sbuf_tensor(name, list(shape), dt))
        env = {
            "nc": nc, "din": din, "scr": scr, "d_out": d_out, "ddbg": ddbg,
            "ops": ops, "debug": debug,
            "ident": SB("g_ident", (128, 128)),
            "ones": SB("g_ones", (128, 128)),
            "onesbf": SB("g_onesbf", (33, 128), BF16),
            "convw": SB("g_convw", (128, 3 * FCH)),
            "convb": SB("g_convb", (128, FCH)),
        }
        for nm in ("ln1g", "ln1b", "td", "mixk", "mixv", "mixr", "ln2g",
                   "ln2b", "tfv", "ew", "omk", "omv", "omr"):
            env[nm] = SB("g_" + nm, (128, KC))
        for nm in ("pen", "xp0", "eml", "emr", "eps", "ln127", "q1c", "q2c"):
            env[nm] = SB("g_" + nm, (128, 1))

        def _reset():
            nc.reset(previous_sbuf_base=nc.sbuf_base,
                     previous_sbuf_top=nc.sbuf_top,
                     previous_psum_base=nc.psum_base,
                     previous_psum_top=nc.psum_top,
                     previous_local_dram_base=nc.local_dram_base,
                     previous_shared_dram_base=nc.shared_dram_base)

        if "P0" in phases:
            _emit_p0(env)
            _reset()
        if "PA" in phases:
            _emit_ffn(env, stage="A")
            _reset()
        if "PB1" in phases:
            _emit_pb1(env)
            _reset()
        if "PB2" in phases:
            _emit_pb2(env)
            _reset()
        if "PC" in phases:
            _emit_ffn(env, stage="C")
            _reset()
        _emit_tail(env, phases)
    return nc


def _emit_p0(env):
    nc = env["nc"]
    din = env["din"]
    scr = env["scr"]
    SCR = env["ops"]["ANT_SCALE_ROUND"]
    from contextlib import ExitStack
    with ExitStack() as ctx:
        SB = lambda name, shape, dt=F32: ctx.enter_context(
            nc.sbuf_tensor(name, list(shape), dt))
        sem = lambda name: ctx.enter_context(nc.semaphore(name))
        wstg = SB("p0_wstg", (128, 2 * HD))
        qstg = SB("p0_qstg", (128, 2 * HD), BF16)
        aw = SB("p0_aw", (128, FCH))
        srow = SB("p0_srow", (1, 256))
        scl = SB("p0_scl", (1, 8))
        hlv = SB("p0_hlv", (1, 8), BF16)
        scbc = SB("p0_scbc", (128, HD))
        s1bc = SB("p0_s1bc", (128, 1))
        s2bc = SB("p0_s2bc", (128, 1))
        p_tr = ctx.enter_context(nc.psum_tensor("p0_ptr", [128, 512], F32))
        sldc = [sem("ldc0"), sem("ldc1")]
        swp = [sem("wp%d" % i) for i in range(4)]
        sq2 = [sem("q2%d" % i) for i in range(4)]
        sstp = [sem("st%d" % i) for i in range(4)]
        sv = sem("v")
        sfold = sem("fold")
        strp = sem("trp")
        ssc = sem("sc")
        sbc = sem("bc")
        sscb = sem("scb")
        sqd = sem("qd")
        s127 = sem("s127")
        snk = sem("snk")
        NW = KC + FCH

        def wsrc(i):
            return (din["w1t"].ap()[i, :, :] if i < KC
                    else din["w2t"].ap()[i - KC, :, :])

        def wwid(i):
            return HD if i < KC else D

        with nc.Block("p0") as blk:
            @blk.sync
            def _(sync):
                consts = ([("ident", env["ident"]), ("convw", env["convw"]),
                           ("convb", env["convb"])]
                          + [(nm, env[nm]) for nm in
                             ("ln1g", "ln1b", "td", "mixk", "mixv", "mixr",
                              "ln2g", "ln2b", "tfv", "pen", "xp0", "eml",
                              "emr")])
                nconst = [0, 0]
                for j, (nm, dst) in enumerate(consts):
                    sync.dma_start(out=dst[:], in_=din[nm].ap()
                                   ).then_inc(sldc[j % 2], 16)
                    nconst[j % 2] += 1
                for i in range(NW):
                    if i >= 2:
                        sync.wait_ge(sv, i - 1)
                    sync.dma_start(
                        out=wstg[:, (i % 2) * HD:(i % 2) * HD + wwid(i)],
                        in_=wsrc(i)).then_inc(swp[i % 4], 16)
                for i in range(NW):
                    if i < 2:
                        sync.wait_ge(sv, NW)
                    else:
                        sync.wait_ge(sqd, i - 1)
                    sync.dma_start(
                        out=wstg[:, (i % 2) * HD:(i % 2) * HD + wwid(i)],
                        in_=wsrc(i)).then_inc(sq2[i % 4], 16)

            @blk.vector
            def _(vector):
                vector.memset(env["ones"][:], 1.0)
                vector.memset(env["onesbf"][:, :], 1.0)
                vector.memset(env["eps"][:], EPS_LN)
                vector.memset(env["ln127"][:], 127.0
                              ).then_inc(s127, 1)
                vector.wait_ge(sldc[0], 16 * 8)
                vector.wait_ge(sldc[1], 16 * 8)
                vector.tensor_scalar(env["omk"][:], env["mixk"][:],
                                     -1.0, 1.0, ALU.mult, ALU.add)
                vector.tensor_scalar(env["omv"][:], env["mixv"][:],
                                     -1.0, 1.0, ALU.mult, ALU.add)
                vector.tensor_scalar(env["omr"][:], env["mixr"][:],
                                     -1.0, 1.0, ALU.mult, ALU.add)
                for wi in range(2):
                    nchunk = KC if wi == 0 else FCH
                    base = 0 if wi == 0 else KC
                    for c in range(nchunk):
                        i = base + c
                        vector.wait_ge(swp[i % 4], 16 * (i // 4 + 1))
                        vector.tensor_reduce(
                            aw[:, c:c + 1],
                            wstg[:, (i % 2) * HD:(i % 2) * HD + wwid(i)],
                            AXL.X, ALU.add, apply_absolute_value=True
                        ).then_inc(sv, 1)
                    vector.drain()
                    vector.tensor_reduce(aw[:, 0:1], aw[:, 0:nchunk],
                                         AXL.X, ALU.add).then_inc(sfold, 1)
                    vector.wait_ge(strp, wi + 1)
                    vector.tensor_copy(srow[0:1, 0:128], p_tr[0:1, 0:128])
                    vector.drain()
                    vector.tensor_reduce(scl[0:1, 0:1], srow[0:1, 0:128],
                                         AXL.X, ALU.add)
                    vector.drain()
                    vector.tensor_scalar(scl[0:1, 1:2], scl[0:1, 0:1],
                                         1.0 / (D * HD), EPS_LN,
                                         ALU.mult, ALU.max)
                    vector.drain()
                    vector.reciprocal(scl[0:1, 2:3], scl[0:1, 1:2])
                    vector.drain()
                    vector.tensor_copy(hlv[0:1, 0:2], scl[0:1, 1:3])
                    vector.drain()
                    vector.scalar_tensor_tensor(
                        hlv[0:1, 2:4], scl[0:1, 1:3], 0.0, hlv[0:1, 0:2],
                        ALU.add, ALU.subtract).then_inc(ssc, 1)
                    vector.wait_ge(sbc, wi + 1)
                    if wi == 0:
                        vector.tensor_scalar(env["q1c"][:], p_tr[:, 0:1],
                                             1.0 / 127.0, None, ALU.mult,
                                             ALU.bypass)
                    else:
                        vector.tensor_scalar(env["q2c"][:], p_tr[:, 0:1],
                                             0.5 / 127.0, None, ALU.mult,
                                             ALU.bypass)
                    vector.tensor_copy((s1bc if wi == 0 else s2bc)[:],
                                       p_tr[:, 1:2]).then_inc(sscb, 1)
                for i in range(NW):
                    wid = wwid(i)
                    slot = wstg[:, (i % 2) * HD:(i % 2) * HD + wid]
                    qdst = qstg[:, (i % 2) * HD:(i % 2) * HD + wid]
                    if i == 0:
                        vector.memset(scbc[:], 1.0)
                        vector.drain()
                        vector.tensor_scalar(scbc[:], scbc[:], s1bc[:, 0:1],
                                             None, ALU.mult, ALU.bypass)
                        vector.drain()
                    if i == KC:
                        vector.memset(scbc[:, 0:D], 1.0)
                        vector.drain()
                        vector.tensor_scalar(scbc[:, 0:D], scbc[:, 0:D],
                                             s2bc[:, 0:1], None, ALU.mult,
                                             ALU.bypass)
                        vector.drain()
                    vector.wait_ge(sq2[i % 4], 16 * (i // 4 + 1))
                    if i >= 2:
                        vector.wait_ge(sstp[(i - 2) % 4],
                                       16 * ((i - 2) // 4 + 1))
                    vector._custom_dve(SCR, out=qdst, in0=slot,
                                       in1=scbc[:, 0:wid], s0=RC2, s1=RC1)
                    vector.drain()
                    vector.tensor_scalar(qdst, qdst, -1.0, 1.0,
                                         ALU.max, ALU.min).then_inc(sqd, 1)

            @blk.tensor
            def _(tensor):
                for wi in range(2):
                    tensor.wait_ge(sfold, wi + 1)
                    tensor.transpose(p_tr[0:1, 0:128], aw[:, 0:1],
                                     env["ident"][:]).then_inc(strp, 1)
                    tensor.wait_ge(ssc, wi + 1)
                    tensor.matmul(p_tr[:, 0:2], env["onesbf"][0:1, :],
                                  hlv[0:1, 0:2], start=True, stop=False)
                    tensor.matmul(p_tr[:, 0:2], env["onesbf"][0:1, :],
                                  hlv[0:1, 2:4], start=False, stop=True
                                  ).then_inc(sbc, 1)

            @blk.scalar
            def _(scalar):
                scalar.wait_ge(s127, 1)
                scalar.activation(env["ln127"][:], env["ln127"][:], ACT.Ln)
                scalar.drain()
                scalar.wait_ge(sldc[0], 16 * 8)
                scalar.wait_ge(sldc[1], 16 * 8)
                scalar.activation(env["ew"][:], env["td"][:], ACT.Exp)
                scalar.drain()
                scalar.activation(env["ew"][:], env["ew"][:], ACT.Exp,
                                  scale=-1.0)

            @blk.gpsimd
            def _(gpsimd):
                for i in range(NW):
                    wid = wwid(i)
                    gpsimd.wait_ge(sqd, i + 1)
                    dst = (scr["w1q"].ap()[i, :, :] if i < KC
                           else scr["w2q"].ap()[i - KC, :, :])
                    gpsimd.dma_start(
                        out=dst, in_=qstg[:, (i % 2) * HD:(i % 2) * HD + wid]
                    ).then_inc(sstp[i % 4], 16)
                for p in range(4):
                    n = len([i for i in range(NW) if i % 4 == p])
                    gpsimd.wait_ge(sstp[p], 16 * n)


def _emit_ffn(env, stage):
    nc = env["nc"]
    din = env["din"]
    scr = env["scr"]
    ddbg = env["ddbg"]
    debug = env["debug"]
    SCR = env["ops"]["ANT_SCALE_ROUND"]
    AX2 = env["ops"]["ANT_AXPY2"]
    g_ones, g_onesbf, g_ident = env["ones"], env["onesbf"], env["ident"]
    g_convw, g_convb = env["convw"], env["convb"]

    if stage == "A":
        groups = _grp_ranges(0, TE, GA)
        dram_out = scr["xa"]
    else:
        groups = _grp_ranges(LH, LH + TOWN, GA)
        dram_out = env["d_out"]
    if PA_MAXG is not None:
        groups = groups[:PA_MAXG]
    NG = len(groups)

    from contextlib import ExitStack
    with ExitStack() as ctx:
        pfx = "f" + stage
        SB = lambda name, shape, dt=F32: ctx.enter_context(
            nc.sbuf_tensor(pfx + name, list(shape), dt))
        PS = lambda name, shape: ctx.enter_context(
            nc.psum_tensor(pfx + name, list(shape), F32))
        sem = lambda name: ctx.enter_context(nc.semaphore(pfx + name))

        GW = GA + 2
        w1q = SB("w1q", (128, KC * HD), BF16)
        w2q = SB("w2q", (128, FCH * D), BF16)
        x_g = SB("x", (128, 2 * KC * GW))
        z_g = SB("z", (128, KC * GW))
        xq_g = SB("xq", (128, KC * GW), BF16)
        h_g = SB("h", (128, FCH * GA))
        dq_t = SB("dq", (128, 2 * GW))
        sq_t = dq_t
        ctp = SB("ct", (128, 2 * GW))
        ct2 = dq_t
        hq_t = SB("hq", (128, 2 * GA), BF16)
        xa_t = SB("xo", (128, KC * GA))
        m_bc = SB("mbc", (128, GW))
        rstd = SB("rst", (128, GW))
        gam = SB("gam", (128, GW))
        u_sb = SB("usb", (128, GW))
        q1 = SB("q1", (128, GW))
        u2_sb = u_sb
        q2 = m_bc
        gamh = gam
        rm = SB("rm", (128, GW))
        amx = SB("amx", (128, 4))
        row4 = SB("r4", (4, 128))
        arow = SB("arw", (1, 512))
        hlr = SB("hlr", (1, 512), BF16)
        hlr2 = SB("hl2", (1, 512), BF16)
        var_t = q1

        p_misc = PS("pm", (128, 1024))
        p_h = PS("ph", (128, 1024))
        p_y = PS("py", (128, 2048))

        def pdc(dc, n):
            if dc < 4:
                return p_y[:, dc * 512:dc * 512 + n]
            if dc < 6:
                return p_h[:, (dc - 4) * 512:(dc - 4) * 512 + n]
            return p_misc[:, (dc - 6) * 512:(dc - 6) * 512 + n]

        sxg = [sem("sxg%d" % i) for i in range(NG)]
        sxasg = [sem("sxas%d" % i) for i in range(NG)]
        swlp = [sem("swlp%d" % i) for i in range(4)]
        ssq = sem("ssq"); ssmm = sem("ssmm")
        svar = sem("svar"); srs = sem("srs")
        srm = sem("srm"); stp = sem("stp"); sam = sem("sam"); st2 = sem("st2")
        sr3 = sem("sr3"); srw = sem("srw"); shl = sem("shl"); sbc = sem("sbc")
        sga = sem("sga"); sxq = sem("sxq")
        sm1p = [sem("sm1a"), sem("sm1b")]
        scvp = [sem("scva"), sem("scvb")]
        sa2p = [sem("sa2a"), sem("sa2b")]
        sslp = [sem("ssla"), sem("sslb")]
        shqp = [sem("shqa"), sem("shqb")]
        sm2p = [sem("sm2a"), sem("sm2b")]
        srmh = sem("srmh")
        stph = sem("stph"); samh = sem("samh"); st2h = sem("st2h")
        sr3h = sem("sr3h"); srwh = sem("srwh"); shlh = sem("shlh")
        sbch = sem("sbch"); sgh = sem("sgh")
        sm2d = sem("sm2d"); sxa = sem("sxa")
        sp127 = sem("sp127")
        snk = sem("snk")

        def grange(g):
            g0, g1 = groups[g]
            a0, a1 = max(g0 - 1, 0), min(g1 + 1, TE)
            return g0, g1, a0, a1

        with nc.Block(pfx) as blk:
            @blk.sync
            def _(sync):
                if "wload" not in SKIP:
                    for i in range(KC):
                        sync.dma_start(out=w1q[:, i * HD:(i + 1) * HD],
                                       in_=scr["w1q"].ap()[i, :, :]
                                       ).then_inc(swlp[i % 4], 16)
                    for j in range(FCH):
                        sync.dma_start(out=w2q[:, j * D:(j + 1) * D],
                                       in_=scr["w2q"].ap()[j, :, :]
                                       ).then_inc(swlp[(KC + j) % 4], 16)
                for g in range(NG):
                    g0, g1, a0, a1 = grange(g)
                    if g >= 2:
                        sync.wait_ge(sxa, KC * (g - 1))
                    for c in range(KC):
                        src = (din["xfm"].ap()[c, :, a0:a1] if stage == "A"
                               else scr["xb"].ap()[c, :, a0:a1])
                        sync.dma_start(
                            out=x_g[:, ((g % 2) * KC + c) * GW:
                                    ((g % 2) * KC + c) * GW + (a1 - a0)],
                            in_=src).then_inc(sxg[g], 16)

            @blk.vector
            def _(vector):
                vector.memset(amx[:], 0.0)
                for g in range(NG):
                    g0, g1, a0, a1 = grange(g)
                    w = a1 - a0
                    gw = g1 - g0
                    xb = (g % 2) * KC * GW
                    nt = (w + 127) // 128
                    nth = (gw + 127) // 128
                    vector.wait_ge(sxg[g], 16 * KC)
                    for c in ([] if "stats" in SKIP else range(KC)):
                        if c >= 2:
                            vector.wait_ge(ssmm, KC * g + c - 1)
                        vector.tensor_tensor(
                            sq_t[:, (c % 2) * GW:(c % 2) * GW + w],
                            x_g[:, xb + c * GW:xb + c * GW + w],
                            x_g[:, xb + c * GW:xb + c * GW + w],
                            ALU.mult).then_inc(ssq, 1)
                    if g >= 1 and "seng" not in SKIP:
                        vector.wait_ge(srs, g)
                    if "stats" not in SKIP:
                        vector.wait_ge(ssmm, KC * (g + 1))
                        vector.tensor_scalar(m_bc[:, 0:w], p_misc[:, 0:w],
                                             1.0 / D, None, ALU.mult,
                                             ALU.bypass)
                        vector.drain()
                        vector.tensor_tensor(var_t[:, 0:w], m_bc[:, 0:w],
                                             m_bc[:, 0:w], ALU.mult)
                        vector.drain()
                        vector.scalar_tensor_tensor(
                            var_t[:, 0:w], p_misc[:, 512:512 + w], 1.0 / D,
                            var_t[:, 0:w], ALU.mult, ALU.subtract
                            ).then_inc(svar, 1)
                    else:
                        vector.memset(m_bc[:, 0:w], 0.0)
                        vector.memset(var_t[:, 0:w], 1.0)
                        vector.drain()
                        vector.engine_nop().then_inc(svar, 1)
                    for c in ([] if "xq" in SKIP else range(KC)):
                        vector.tensor_tensor(
                            z_g[:, c * GW:c * GW + w],
                            x_g[:, xb + c * GW:xb + c * GW + w],
                            m_bc[:, 0:w], ALU.subtract)
                    vector.drain()
                    if "amax" not in SKIP:
                        vector.tensor_reduce(
                            rm[:, 0:w],
                            sap(z_g, [[KC * GW, 128], [1, w], [GW, KC]]),
                            AXL.X, ALU.max, apply_absolute_value=True
                            ).then_inc(srm, 1)
                    if "amax" not in SKIP and "amx2" not in SKIP:
                        vector.wait_ge(stp, g + 1)
                        for t in range(nt):
                            tw = min(128, w - t * 128)
                            last = vector.tensor_reduce(
                                amx[0:tw, t:t + 1],
                                p_misc[0:tw, t * 128:t * 128 + 128],
                                AXL.X, ALU.max)
                        last.then_inc(sam, 1)
                        vector.wait_ge(st2, g + 1)
                        if g >= 1 and "amx3" not in SKIP:
                            vector.wait_ge(srwh, 16 * g)
                        vector.tensor_copy(row4[0:4, :], p_misc[0:4, 384:512]
                                           ).then_inc(sr3, 1)
                    if ("amax" not in SKIP and "amx2" not in SKIP
                            and "amx3" not in SKIP):
                        vector.wait_ge(srw, 16 * (g + 1))
                        vector.tensor_copy(hlr[0:1, 0:w], arow[0:1, 0:w])
                        vector.drain()
                        vector.scalar_tensor_tensor(
                            hlr2[0:1, 0:w], arow[0:1, 0:w], 0.0,
                            hlr[0:1, 0:w],
                            ALU.add, ALU.subtract).then_inc(shl, 1)
                    if ("amax" not in SKIP and "amx2" not in SKIP
                            and "amx3" not in SKIP and "amx4" not in SKIP):
                        vector.wait_ge(sbc, g + 1)
                        vector.wait_ge(srs, g + 1)
                        vector.tensor_tensor(u_sb[:, 0:w],
                                             p_misc[:, 512:512 + w],
                                             rstd[:, 0:w], ALU.mult)
                        vector.drain()
                        vector.tensor_scalar(u_sb[:, 0:w], u_sb[:, 0:w], 1e-5,
                                             None, ALU.max, ALU.bypass)
                        vector.drain()
                        vector.tensor_scalar(q1[:, 0:w], u_sb[:, 0:w],
                                             env["q1c"][:, 0:1], None,
                                             ALU.mult, ALU.bypass
                                             ).then_inc(sga, 1)
                        vector.wait_ge(sp127, 2 * g + 1)
                        vector.tensor_tensor(gam[:, 0:w], u_sb[:, 0:w],
                                             rstd[:, 0:w], ALU.mult)
                        vector.drain()
                    if ("amax" in SKIP or "amx2" in SKIP
                            or "amx3" in SKIP or "amx4" in SKIP):
                        if "seng" not in SKIP:
                            vector.wait_ge(srs, g + 1)
                        vector.memset(q1[:, 0:w], 0.05)
                        vector.memset(gam[:, 0:w], 20.0)
                        vector.drain()
                    if "xq" in SKIP:
                        last = vector.memset(xq_g[:, 0:KC * GW], 1.0)
                        last.then_inc(sxq, 1)
                    else:
                      last = None
                      for c in range(KC):
                        last = vector._custom_dve(
                            SCR, out=xq_g[:, c * GW:c * GW + w],
                            in0=z_g[:, c * GW:c * GW + w],
                            in1=gam[:, 0:w], s0=RC2, s1=RC1)
                    for col, msk in ([] if "xq" in SKIP else
                                     ((LH - 1, "eml"), (LH + TOWN, "emr"))):
                        if a0 <= col < a1:
                            vector.drain()
                            last = vector.engine_nop()
                            j = col - a0
                            for c in range(KC):
                                last = vector.tensor_scalar(
                                    xq_g[:, c * GW + j:c * GW + j + 1],
                                    xq_g[:, c * GW + j:c * GW + j + 1],
                                    env[msk][:, 0:1], None, ALU.mult,
                                    ALU.bypass)
                    if "xq" not in SKIP:
                        last.then_inc(sxq, 1)
                    c0 = g0 - a0
                    c1v = c0 + gw
                    lo = c0 if a0 < g0 else c0 + 1
                    hi = c1v if a1 > g1 else c1v - 1
                    for fc in ([] if "mm1" in SKIP else range(FCH)):
                        sl = fc % 2
                        o = sl * GW
                        vector.wait_ge(sm1p[g % 2], FCH * (g // 2) + ((fc + 2) & ~1))
                        if fc >= 2:
                            vector.wait_ge(sslp[g % 2],
                                           FCH * (g // 2) + (fc & ~1))
                        i = vector.tensor_tensor(dq_t[:, o:o + w],
                                                 p_h[:, sl * 512:sl * 512 + w],
                                                 q1[:, 0:w], ALU.mult)
                        if fc % 2 == 1:
                            i.then_inc(scvp[g % 2], 2)
                        vector.drain()
                        if "conv" in SKIP:
                            if fc % 2 == 1:
                                vector.engine_nop().then_inc(sa2p[g % 2], 2)
                            continue
                        cw0 = g_convw[:, 3 * fc + 0:3 * fc + 1]
                        cw1 = g_convw[:, 3 * fc + 1:3 * fc + 2]
                        cw2 = g_convw[:, 3 * fc + 2:3 * fc + 3]
                        vector._custom_dve(
                            AX2, out=ctp[:, o + lo:o + hi],
                            in0=dq_t[:, o + lo - 1:o + hi - 1],
                            in1=dq_t[:, o + lo + 1:o + hi + 1],
                            s0=cw0, s1=cw2)
                        vector.drain()
                        if lo > c0:
                            vector.tensor_scalar(
                                ctp[:, o + c0:o + c0 + 1],
                                dq_t[:, o + c0 + 1:o + c0 + 2],
                                cw2, None, ALU.mult, ALU.bypass)
                            vector.drain()
                            vector.scalar_tensor_tensor(
                                ct2[:, o + c0:o + c0 + 1],
                                dq_t[:, o + c0:o + c0 + 1], cw1,
                                ctp[:, o + c0:o + c0 + 1], ALU.mult, ALU.add)
                        if hi < c1v:
                            vector.tensor_scalar(
                                ctp[:, o + c1v - 1:o + c1v],
                                dq_t[:, o + c1v - 2:o + c1v - 1],
                                cw0, None, ALU.mult, ALU.bypass)
                            vector.drain()
                            vector.scalar_tensor_tensor(
                                ct2[:, o + c1v - 1:o + c1v],
                                dq_t[:, o + c1v - 1:o + c1v], cw1,
                                ctp[:, o + c1v - 1:o + c1v], ALU.mult, ALU.add)
                        lastc = vector.scalar_tensor_tensor(
                            ct2[:, o + lo:o + hi],
                            dq_t[:, o + lo:o + hi], cw1,
                            ctp[:, o + lo:o + hi], ALU.mult, ALU.add)
                        (lastc.then_inc(sa2p[g % 2], 2) if fc % 2 == 1 else None)
                    if "mm1" not in SKIP:
                        vector.wait_ge(sslp[g % 2], FCH * (g // 2 + 1))
                    if "amax" not in SKIP:
                        vector.tensor_reduce(
                            rm[:, 0:gw],
                            sap(h_g, [[FCH * GA, 128], [1, gw], [GA, FCH]]),
                            AXL.X, ALU.max, apply_absolute_value=True
                            ).then_inc(srmh, 1)
                    if "amax" not in SKIP and "amx2" not in SKIP:
                        vector.wait_ge(stph, g + 1)
                        for t in range(nth):
                            tw = min(128, gw - t * 128)
                            last = vector.tensor_reduce(
                                amx[0:tw, t:t + 1],
                                p_misc[0:tw, t * 128:t * 128 + 128],
                                AXL.X, ALU.max)
                        last.then_inc(samh, 1)
                        vector.wait_ge(st2h, g + 1)
                        vector.tensor_copy(row4[0:4, :], p_misc[0:4, 384:512]
                                           ).then_inc(sr3h, 1)
                    if ("amax" not in SKIP and "amx2" not in SKIP
                            and "amx3" not in SKIP):
                        vector.wait_ge(srwh, 16 * (g + 1))
                        vector.tensor_copy(hlr[0:1, 0:gw], arow[0:1, 0:gw])
                        vector.drain()
                        vector.scalar_tensor_tensor(
                            hlr2[0:1, 0:gw], arow[0:1, 0:gw], 0.0,
                            hlr[0:1, 0:gw],
                            ALU.add, ALU.subtract).then_inc(shlh, 1)
                    if ("amax" not in SKIP and "amx2" not in SKIP
                            and "amx3" not in SKIP and "amx4" not in SKIP):
                        vector.wait_ge(sbch, g + 1)
                        vector.tensor_scalar(u2_sb[:, 0:gw],
                                             p_misc[:, 512:512 + gw], 1e-5,
                                             None, ALU.max, ALU.bypass)
                        vector.drain()
                        vector.tensor_scalar(q2[:, 0:gw], u2_sb[:, 0:gw],
                                             env["q2c"][:, 0:1], None,
                                             ALU.mult, ALU.bypass
                                             ).then_inc(sgh, 1)
                        vector.wait_ge(sp127, 2 * g + 2)
                    if ("amax" in SKIP or "amx2" in SKIP
                            or "amx3" in SKIP or "amx4" in SKIP):
                        vector.memset(q2[:, 0:gw], 0.05)
                        vector.memset(gamh[:, 0:gw], 20.0)
                        vector.drain()
                        vector.engine_nop().then_inc(sgh, 1)
                    if "hqmm2" not in SKIP:
                        for fc in range(FCH):
                            if fc >= 2:
                                vector.wait_ge(sm2p[g % 2],
                                               FCH * (g // 2) + (fc & ~1))
                            i = vector._custom_dve(
                                SCR,
                                out=hq_t[:, (fc % 2) * GA:(fc % 2) * GA + gw],
                                in0=h_g[:, fc * GA:fc * GA + gw],
                                in1=gamh[:, 0:gw], s0=RC2, s1=RC1)
                            if fc % 2 == 1:
                                i.then_inc(shqp[g % 2], 2)
                    if g >= 1 and "stores" not in SKIP:
                        vector.wait_ge(sxasg[g - 1], 16 * KC)
                    for dc in range(KC):
                        if "hqmm2" not in SKIP:
                            if dc < KC - 1:
                                vector.wait_ge(sm2d, (KC - 1) * g + dc + 1)
                            else:
                                vector.wait_ge(sm2p[g % 2],
                                               FCH * (g // 2 + 1))
                            sl = dc
                            vector.tensor_tensor(
                                xa_t[:, sl * GA:sl * GA + gw],
                                pdc(dc, gw), q2[:, 0:gw], ALU.mult)
                            vector.drain()
                            last = vector.tensor_tensor(
                                xa_t[:, sl * GA:sl * GA + gw],
                                xa_t[:, sl * GA:sl * GA + gw],
                                x_g[:, xb + dc * GW + (g0 - a0):
                                    xb + dc * GW + (g0 - a0) + gw], ALU.add)
                        else:
                            sl = dc
                            last = vector.memset(
                                xa_t[:, sl * GA:sl * GA + gw], 0.5)
                        last.then_inc(sxa, 1)

            @blk.tensor
            def _(tensor):
                if "wload" not in SKIP:
                    for p in range(4):
                        n = len([i for i in range(KC) if i % 4 == p]) + \
                            len([j for j in range(FCH) if (KC + j) % 4 == p])
                        tensor.wait_ge(swlp[p], 16 * n)
                for g in range(NG):
                    g0, g1, a0, a1 = grange(g)
                    w = a1 - a0
                    gw = g1 - g0
                    xb = (g % 2) * KC * GW
                    nt = (w + 127) // 128
                    nth = (gw + 127) // 128
                    tensor.wait_ge(sxg[g], 16 * KC)
                    if g >= 1:
                        tensor.wait_ge(sxa, KC * g)
                    if "stats" not in SKIP:
                        for c in range(KC):
                            tensor.matmul(p_misc[:, 0:w], g_ones[:],
                                          x_g[:, xb + c * GW:xb + c * GW + w],
                                          start=(c == 0), stop=(c == KC - 1))
                        last = None
                        for c in range(KC):
                            tensor.wait_ge(ssq, KC * g + c + 1)
                            last = tensor.matmul(
                                p_misc[:, 512:512 + w], g_ones[:],
                                sq_t[:, (c % 2) * GW:(c % 2) * GW + w],
                                start=(c == 0), stop=(c == KC - 1))
                            last.then_inc(ssmm, 1)
                    if "amax" not in SKIP and "amx2" not in SKIP:
                        tensor.wait_ge(srm, g + 1)
                        for t in range(nt):
                            tw = min(128, w - t * 128)
                            last = tensor.transpose(
                                p_misc[0:tw, t * 128:t * 128 + 128],
                                rm[:, t * 128:t * 128 + tw], g_ident[:])
                        last.then_inc(stp, 1)
                        tensor.wait_ge(sam, g + 1)
                        tensor.transpose(p_misc[0:4, 384:512], amx[:, 0:4],
                                         g_ident[:]).then_inc(st2, 1)
                    if ("amax" not in SKIP and "amx2" not in SKIP
                            and "amx3" not in SKIP and "amx4" not in SKIP):
                        tensor.wait_ge(shl, g + 1)
                        tensor.matmul(p_misc[:, 512:512 + w],
                                      g_onesbf[0:1, :],
                                      hlr[0:1, 0:w], start=True, stop=False)
                        tensor.matmul(p_misc[:, 512:512 + w],
                                      g_onesbf[0:1, :],
                                      hlr2[0:1, 0:w], start=False, stop=True
                                      ).then_inc(sbc, 1)
                    tensor.wait_ge(sxq, g + 1)
                    for fc in ([] if "mm1" in SKIP else range(FCH)):
                        sl = fc % 2
                        if fc >= 2:
                            tensor.wait_ge(scvp[g % 2],
                                           FCH * (g // 2) + (fc & ~1))
                        lastm = None
                        for c in range(KC):
                            lastm = tensor.matmul(
                                p_h[:, sl * 512:sl * 512 + w],
                                w1q[:, c * HD + fc * 128:
                                    c * HD + fc * 128 + 128],
                                xq_g[:, c * GW:c * GW + w],
                                start=(c == 0), stop=(c == KC - 1))
                        (lastm.then_inc(sm1p[g % 2], 2) if fc % 2 == 1 else None)
                    if "amax" not in SKIP and "amx2" not in SKIP:
                        tensor.wait_ge(srmh, g + 1)
                        for t in range(nth):
                            tw = min(128, gw - t * 128)
                            last = tensor.transpose(
                                p_misc[0:tw, t * 128:t * 128 + 128],
                                rm[:, t * 128:t * 128 + tw], g_ident[:])
                        last.then_inc(stph, 1)
                        tensor.wait_ge(samh, g + 1)
                        tensor.transpose(p_misc[0:4, 384:512], amx[:, 0:4],
                                         g_ident[:]).then_inc(st2h, 1)
                    if ("amax" not in SKIP and "amx2" not in SKIP
                            and "amx3" not in SKIP and "amx4" not in SKIP):
                        tensor.wait_ge(shlh, g + 1)
                        tensor.matmul(p_misc[:, 512:512 + gw],
                                      g_onesbf[0:1, :],
                                      hlr[0:1, 0:gw], start=True, stop=False)
                        tensor.matmul(p_misc[:, 512:512 + gw],
                                      g_onesbf[0:1, :],
                                      hlr2[0:1, 0:gw], start=False, stop=True
                                      ).then_inc(sbch, 1)
                    if "hqmm2" in SKIP:
                        continue
                    tensor.wait_ge(sgh, g + 1)
                    tensor.wait_ge(scvp[g % 2], FCH * (g // 2 + 1))
                    for fc in range(FCH):
                        tensor.wait_ge(shqp[g % 2], FCH * (g // 2) + ((fc + 2) & ~1))
                        for dc in range(KC):
                            mmi = tensor.matmul(
                                pdc(dc, gw),
                                w2q[:, fc * D + dc * 128:
                                    fc * D + dc * 128 + 128],
                                hq_t[:, (fc % 2) * GA:(fc % 2) * GA + gw],
                                start=(fc == 0), stop=(fc == FCH - 1))
                            if fc == FCH - 1:
                                if dc == KC - 1:
                                    mmi.then_inc(sm2p[g % 2], 2)
                                else:
                                    mmi.then_inc(sm2d, 1)
                            elif dc == KC - 1 and fc % 2 == 1:
                                mmi.then_inc(sm2p[g % 2], 2)

            @blk.scalar
            def _(scalar):
                for g in range(NG):
                    g0, g1, a0, a1 = grange(g)
                    w = a1 - a0
                    gw = g1 - g0
                    if "seng" in SKIP:
                        continue
                    scalar.wait_ge(svar, g + 1)
                    if "stats" not in SKIP:
                        scalar.activation(rstd[:, 0:w], var_t[:, 0:w], ACT.Ln,
                                          bias=env["eps"][:, 0:1])
                        scalar.drain()
                        scalar.activation(rstd[:, 0:w], rstd[:, 0:w], ACT.Exp,
                                          scale=-0.5).then_inc(srs, 1)
                    else:
                        scalar.activation(rstd[:, 0:w], var_t[:, 0:w],
                                          ACT.Copy).then_inc(srs, 1)
                    if ("amax" not in SKIP and "amx2" not in SKIP
                            and "amx3" not in SKIP and "amx4" not in SKIP):
                        scalar.wait_ge(sga, g + 1)
                        scalar.activation(u_sb[:, 0:w], u_sb[:, 0:w], ACT.Ln)
                        scalar.drain()
                        scalar.activation(u_sb[:, 0:w], u_sb[:, 0:w], ACT.Exp,
                                          scale=-1.0,
                                          bias=env["ln127"][:, 0:1]
                                          ).then_inc(sp127, 1)
                    for fc in ([] if "mm1" in SKIP else range(FCH)):
                        scalar.wait_ge(sa2p[g % 2], FCH * (g // 2) + ((fc + 2) & ~1))
                        i = scalar.activation(
                            h_g[:, fc * GA:fc * GA + gw],
                            ct2[:, (fc % 2) * GW + (g0 - a0):
                                (fc % 2) * GW + (g0 - a0) + gw],
                            ACT.Silu, bias=g_convb[:, fc:fc + 1])
                        if fc % 2 == 1:
                            i.then_inc(sslp[g % 2], 2)
                    if ("amax" not in SKIP and "amx2" not in SKIP
                            and "amx3" not in SKIP and "amx4" not in SKIP):
                        scalar.wait_ge(sgh, g + 1)
                        scalar.activation(gamh[:, 0:gw], u2_sb[:, 0:gw],
                                          ACT.Ln)
                        scalar.drain()
                        scalar.activation(gamh[:, 0:gw], gamh[:, 0:gw],
                                          ACT.Exp, scale=-1.0,
                                          bias=env["ln127"][:, 0:1]
                                          ).then_inc(sp127, 1)

            @blk.gpsimd
            def _(gpsimd):
                for g in range(NG):
                    g0, g1, a0, a1 = grange(g)
                    gw = g1 - g0
                    if ("amax" not in SKIP and "amx2" not in SKIP
                            and "amx3" not in SKIP):
                        gpsimd.wait_ge(sr3, g + 1)
                        gpsimd.dma_start(out=arow[0:1, 0:512],
                                         in_=row4[0:4, :]).then_inc(srw, 16)
                        gpsimd.wait_ge(sr3h, g + 1)
                        gpsimd.dma_start(out=arow[0:1, 0:512],
                                         in_=row4[0:4, :]).then_inc(srwh, 16)
                    for dc in ([] if "stores" in SKIP else range(KC)):
                        gpsimd.wait_ge(sxa, KC * g + dc + 1)
                        if stage == "A":
                            dst = dram_out.ap()[dc, :, g0:g1]
                        else:
                            dst = dram_out.ap()[dc, :, g0 - LH:g1 - LH]
                        gpsimd.dma_start(
                            out=dst, in_=xa_t[:, dc * GA:dc * GA + gw]
                        ).then_inc(sxasg[g], 16)
                for g in ([] if "stores" in SKIP else range(NG)):
                    gpsimd.wait_ge(sxasg[g], 16 * KC)
                if debug and stage == "A":
                    for c in range(KC):
                        gpsimd.dma_start(out=ddbg["dxa"].ap()[c, :, :],
                                         in_=dram_out.ap()[c, :, :]
                                         ).then_inc(snk, 16)
                    gpsimd.wait_ge(snk, 16 * KC)


def _emit_pb1(env):
    nc = env["nc"]
    din = env["din"]
    scr = env["scr"]
    AX2 = env["ops"]["ANT_AXPY2"]
    g_ones = env["ones"]
    from contextlib import ExitStack

    G = GB1
    assert G > LH
    groups = _grp_ranges(0, TE, G)
    NG = len(groups)
    with ExitStack() as ctx:
        SB = lambda name, shape, dt=F32: ctx.enter_context(
            nc.sbuf_tensor("b1" + name, list(shape), dt))
        PS = lambda name, shape: ctx.enter_context(
            nc.psum_tensor("b1" + name, list(shape), F32))
        sem = lambda name: ctx.enter_context(nc.semaphore("b1" + name))

        SW = G + 1
        wstg = SB("wst", (128, 2 * D))
        wk_sb = SB("wk", (128, KC * D), BF16)
        wv_sb = SB("wv", (128, KC * D), BF16)
        wr_sb = SB("wr", (128, KC * D), BF16)
        wo_sb = SB("wo", (128, KC * D), BF16)
        xa_g = SB("xa", (128, 2 * KC * SW))
        xo_g = SB("xo", (128, KC * SW))
        wg_g = SB("wg", (128, KC * SW))
        ek_g = SB("ek", (128, KC * G))
        ekv_g = SB("ekv", (128, KC * G))
        A_g = SB("A", (128, KC * G))
        B_g = SB("B", (128, KC * G))
        mx_bf = [SB("mx%d" % i, (128, KC * G), BF16) for i in range(3)]
        rw_bf = SB("rw", (128, KC * G), BF16)
        ewb = SB("ewb", (128, KC * G))
        sq_t = SB("sq", (128, 2 * SW))
        wkv_t = SB("wkv", (128, G))
        winv_t = SB("wnv", (128, G))
        carA = SB("cA", (128, KC))
        carB = SB("cB", (128, KC))
        m_bc = SB("mbc", (128, SW))
        rstd = SB("rsd", (128, SW))
        m2_bc = SB("m2b", (128, SW))
        rstd2 = SB("rs2", (128, SW))
        var_t = SB("var", (128, SW))
        epsw = SB("epw", (128, 1))
        zero_pp = SB("zpp", (128, 1))
        gmk_t = SB("gmk", (128, 3 * KC))
        gom_t = SB("gom", (128, 3 * KC))
        gom0_t = SB("gm0", (128, 3 * KC))
        bmix_f = SB("bmf", (128, 6 * KC))
        bmix_bf = SB("bmb", (128, 6 * KC), BF16)
        brows = SB("brw", (1, 6 * D), BF16)
        ones_row = SB("onr", (1, G), BF16)
        xp0_row = SB("xpr", (1, G), BF16)

        p_st = PS("pst", (128, 1024))
        p_kv = PS("pkv", (128, 1024))
        p_o = PS("po", (128, 2048))

        swl = [sem("wl%d" % i) for i in range(4)]
        swc = sem("wc")
        scst = sem("cst")
        srowm = sem("rwm")
        srowc = sem("rwc")
        sxg = [sem("xg%d" % i) for i in range(NG)]
        sxsg = [sem("xs%d" % i) for i in range(NG)]
        ssq = sem("ssq"); ssmm = sem("smm"); svar = sem("sva")
        srs = sem("srs")
        ssq2 = sem("sq2"); ssm2 = sem("sm2"); svar2 = sem("sv2")
        smix = sem("smx")
        smk = sem("smk"); smv = sem("smv"); smr = sem("smr")
        sek = sem("sek")
        sekv = sem("skv")
        srg = sem("srg")
        sscn = sem("scn")
        slnb = sem("lnb")
        srw = sem("srw")
        smo = sem("smo")
        scar = sem("car")
        sx1 = sem("sx1")
        snk = sem("snk")

        def grange(g):
            c0, c1 = groups[g]
            a0 = max(c0 - 1, 0)
            return c0, c1, a0, c1

        wmats = (wk_sb, wv_sb, wr_sb, wo_sb)

        with nc.Block("b1") as blk:
            @blk.sync
            def _(sync):
                for i, wdr in enumerate((din["wk"], din["wv"], din["wr"],
                                         din["wo"])):
                    for c in range(KC):
                        j = i * KC + c
                        if j >= 2:
                            sync.wait_ge(swc, j - 1)
                        sync.dma_start(out=wstg[:, (j % 2) * D:(j % 2) * D + D],
                                       in_=wdr.ap()[c, :, :]
                                       ).then_inc(swl[j % 4], 16)
                for g in range(NG):
                    c0, c1, a0, a1 = grange(g)
                    if g >= 2:
                        sync.wait_ge(sx1, KC * (g - 1))
                    for c in range(KC):
                        sync.dma_start(
                            out=xa_g[:, ((g % 2) * KC + c) * SW:
                                    ((g % 2) * KC + c) * SW + (a1 - a0)],
                            in_=scr["xa"].ap()[c, :, a0:a1]
                            ).then_inc(sxg[g], 16)

            @blk.vector
            def _(vector):
                vector.memset(epsw[:], 1e-8)
                vector.memset(zero_pp[:], 0.0)
                vector.memset(carA[:], 0.0)
                vector.memset(carB[:], 0.0)
                vector.memset(ones_row[0:1, :], 1.0)
                vector.memset(xp0_row[0:1, :], 1.0)
                vector.drain()
                vector.tensor_scalar(xp0_row[0:1, LH:LH + 1],
                                     xp0_row[0:1, LH:LH + 1],
                                     env["xp0"][0:1, 0:1], None,
                                     ALU.mult, ALU.bypass)
                for c in range(KC):
                    vector.memset(ewb[:, c * G:(c + 1) * G], 1.0)
                vector.drain()
                for c in range(KC):
                    vector.tensor_scalar(ewb[:, c * G:(c + 1) * G],
                                         ewb[:, c * G:(c + 1) * G],
                                         env["ew"][:, c:c + 1], None,
                                         ALU.mult, ALU.bypass)
                for mi, mx in enumerate(("mixk", "mixv", "mixr")):
                    om = ("omk", "omv", "omr")[mi]
                    vector.tensor_tensor(gmk_t[:, mi * KC:(mi + 1) * KC],
                                         env["ln1g"][:], env[mx][:], ALU.mult)
                    vector.tensor_tensor(gom_t[:, mi * KC:(mi + 1) * KC],
                                         env["ln1g"][:], env[om][:], ALU.mult)
                    vector.tensor_tensor(bmix_f[:, mi * KC:(mi + 1) * KC],
                                         env["ln1b"][:], env[mx][:], ALU.mult)
                    vector.tensor_tensor(
                        bmix_f[:, (3 + mi) * KC:(4 + mi) * KC],
                        env["ln1b"][:], env[om][:], ALU.mult)
                vector.drain()
                for mi in range(3):
                    vector.tensor_scalar(gom0_t[:, mi * KC:(mi + 1) * KC],
                                         gom_t[:, mi * KC:(mi + 1) * KC],
                                         env["xp0"][:, 0:1], None,
                                         ALU.mult, ALU.bypass)
                vector.tensor_copy(bmix_bf[:, :], bmix_f[:, :]
                                   ).then_inc(scst, 1)
                for j in range(4 * KC):
                    dst = wmats[j // KC]
                    c = j % KC
                    vector.wait_ge(swl[j % 4], 16 * (j // 4 + 1))
                    vector.tensor_copy(dst[:, c * D:(c + 1) * D],
                                       wstg[:, (j % 2) * D:(j % 2) * D + D]
                                       ).then_inc(swc, 1)
                rord = [(var * 3 + mi) * 2 + half for mi in range(3)
                        for var in range(2) for half in range(2)]
                for i, ridx in enumerate(rord):
                    vector.wait_ge(srowm, i + 1)
                    vector.tensor_copy(
                        brows[0:1, ridx * 512:(ridx + 1) * 512],
                        p_st[0:1, 0:512]).then_inc(srowc, 1)
                for g in range(NG):
                    c0, c1, a0, a1 = grange(g)
                    sw = a1 - a0
                    cw = c1 - c0
                    xb = (g % 2) * KC * SW
                    off = c0 - a0
                    vector.wait_ge(sxg[g], 16 * KC)
                    for c in range(KC):
                        if c >= 2:
                            vector.wait_ge(ssmm, KC * g + c - 1)
                        vector.tensor_tensor(
                            sq_t[:, (c % 2) * SW:(c % 2) * SW + sw],
                            xa_g[:, xb + c * SW:xb + c * SW + sw],
                            xa_g[:, xb + c * SW:xb + c * SW + sw],
                            ALU.mult).then_inc(ssq, 1)
                    if g >= 1:
                        vector.wait_ge(srs, 2 * g)
                    vector.wait_ge(ssmm, KC * (g + 1))
                    vector.tensor_scalar(m_bc[:, 0:sw], p_st[:, 0:sw],
                                         1.0 / D, None, ALU.mult, ALU.bypass)
                    vector.drain()
                    vector.tensor_tensor(var_t[:, 0:sw], m_bc[:, 0:sw],
                                         m_bc[:, 0:sw], ALU.mult)
                    vector.drain()
                    vector.scalar_tensor_tensor(
                        var_t[:, 0:sw], p_st[:, 512:512 + sw], 1.0 / D,
                        var_t[:, 0:sw], ALU.mult, ALU.subtract
                        ).then_inc(svar, 1)
                    vector.wait_ge(srs, 2 * g + 1)
                    if g >= 1:
                        vector.wait_ge(sxsg[g - 1], 16 * KC)
                    for c in range(KC):
                        vector.tensor_tensor(
                            xo_g[:, c * SW:c * SW + sw],
                            xa_g[:, xb + c * SW:xb + c * SW + sw],
                            m_bc[:, 0:sw], ALU.subtract)
                    vector.drain()
                    for c in range(KC):
                        vector.tensor_tensor(
                            xo_g[:, c * SW:c * SW + sw],
                            xo_g[:, c * SW:c * SW + sw],
                            rstd[:, 0:sw], ALU.mult)
                    vector.drain()
                    for c in range(KC):
                        if c >= 2:
                            vector.wait_ge(ssm2, KC * g + c - 1)
                        vector.tensor_tensor(
                            sq_t[:, (c % 2) * SW:(c % 2) * SW + sw],
                            xo_g[:, c * SW:c * SW + sw],
                            xo_g[:, c * SW:c * SW + sw],
                            ALU.mult).then_inc(ssq2, 1)
                    vector.wait_ge(ssm2, KC * (g + 1))
                    vector.tensor_scalar(m2_bc[:, 0:sw], p_st[:, 0:sw],
                                         1.0 / D, None, ALU.mult, ALU.bypass)
                    vector.drain()
                    vector.tensor_tensor(var_t[:, 0:sw], m2_bc[:, 0:sw],
                                         m2_bc[:, 0:sw], ALU.mult)
                    vector.drain()
                    vector.scalar_tensor_tensor(
                        var_t[:, 0:sw], p_st[:, 512:512 + sw], 1.0 / D,
                        var_t[:, 0:sw], ALU.mult, ALU.subtract
                        ).then_inc(svar2, 1)
                    vector.wait_ge(srs, 2 * g + 2)
                    for c in range(KC):
                        vector.tensor_tensor(
                            wg_g[:, c * SW:c * SW + sw],
                            xo_g[:, c * SW:c * SW + sw],
                            m2_bc[:, 0:sw], ALU.subtract)
                    vector.drain()
                    for c in range(KC):
                        vector.tensor_tensor(
                            wg_g[:, c * SW:c * SW + sw],
                            wg_g[:, c * SW:c * SW + sw],
                            rstd2[:, 0:sw], ALU.mult)
                    vector.drain()
                    for mi in range(3):
                        mmsem = (smk, smv, smr)[mi]
                        if g >= 1:
                            vector.wait_ge(mmsem, KC * g)
                        for c in range(KC):
                            gmk = gmk_t[:, mi * KC + c:mi * KC + c + 1]
                            gom = gom_t[:, mi * KC + c:mi * KC + c + 1]
                            dst = mx_bf[mi]
                            if off == 0:
                                vector.tensor_scalar(
                                    dst[:, c * G:c * G + 1],
                                    wg_g[:, c * SW:c * SW + 1],
                                    gmk, None, ALU.mult, ALU.bypass)
                                last = vector._custom_dve(
                                    AX2, out=dst[:, c * G + 1:c * G + cw],
                                    in0=wg_g[:, c * SW + 1:c * SW + cw],
                                    in1=wg_g[:, c * SW:c * SW + cw - 1],
                                    s0=gmk, s1=gom)
                            else:
                                last = vector._custom_dve(
                                    AX2, out=dst[:, c * G:c * G + cw],
                                    in0=wg_g[:, c * SW + 1:c * SW + 1 + cw],
                                    in1=wg_g[:, c * SW:c * SW + cw],
                                    s0=gmk, s1=gom)
                            if c0 <= LH < c1:
                                j = LH - c0
                                gom0 = gom0_t[:, mi * KC + c:mi * KC + c + 1]
                                vector.drain()
                                last = vector._custom_dve(
                                    AX2, out=dst[:, c * G + j:c * G + j + 1],
                                    in0=wg_g[:, c * SW + off + j:
                                             c * SW + off + j + 1],
                                    in1=wg_g[:, c * SW + off + j - 1:
                                             c * SW + off + j],
                                    s0=gmk, s1=gom0)
                            last.then_inc(smix, 1)
                    for oc in range(KC):
                        vector.wait_ge(sek, KC * g + oc + 1)
                        vector.wait_ge(smv, KC * g + oc + 1)
                        vector.tensor_tensor(
                            ekv_g[:, oc * G:oc * G + cw],
                            ek_g[:, oc * G:oc * G + cw],
                            p_kv[:, 512:512 + cw],
                            ALU.mult).then_inc(sekv, 1)
                    vector.drain()
                    for oc in range(KC):
                        if g >= 1:
                            vector.wait_ge(srw, KC * (g - 1) + oc + 1)
                        vector.tensor_tensor_scan(
                            A_g[:, oc * G:oc * G + cw],
                            ewb[:, oc * G:oc * G + cw],
                            ekv_g[:, oc * G:oc * G + cw],
                            carA[:, oc:oc + 1], ALU.mult, ALU.add)
                        vector.tensor_tensor_scan(
                            B_g[:, oc * G:oc * G + cw],
                            ewb[:, oc * G:oc * G + cw],
                            ek_g[:, oc * G:oc * G + cw],
                            carB[:, oc:oc + 1], ALU.mult, ALU.add
                            ).then_inc(sscn, 1)
                    vector.drain()
                    vector.tensor_copy(carA[:, 0:KC],
                                       sap(A_g, [[KC * G, 128], [G, KC]],
                                           offset=cw - 1))
                    vector.tensor_copy(carB[:, 0:KC],
                                       sap(B_g, [[KC * G, 128], [G, KC]],
                                           offset=cw - 1)).then_inc(scar, 1)
                    vector.drain()
                    for oc in range(KC):
                        vector.wait_ge(slnb, KC * g + oc + 1)
                        vector.wait_ge(srg, KC * g + oc + 1)
                        if g >= 1:
                            vector.wait_ge(smo, KC * g)
                        vector.tensor_tensor(wkv_t[:, 0:cw],
                                             A_g[:, oc * G:oc * G + cw],
                                             ek_g[:, oc * G:oc * G + cw],
                                             ALU.mult)
                        vector.drain()
                        vector.tensor_tensor(rw_bf[:, oc * G:oc * G + cw],
                                             wkv_t[:, 0:cw],
                                             B_g[:, oc * G:oc * G + cw],
                                             ALU.mult).then_inc(srw, 1)
                        vector.drain()
                    for dc in range(KC):
                        vector.wait_ge(smo, KC * g + dc + 1)
                        vector.tensor_tensor(
                            xo_g[:, dc * SW + off:dc * SW + off + cw],
                            p_o[:, (dc % 4) * 512:(dc % 4) * 512 + cw],
                            xo_g[:, dc * SW + off:dc * SW + off + cw],
                            ALU.add).then_inc(sx1, 1)

            @blk.tensor
            def _(tensor):
                i = 0
                for mi in range(3):
                    for var in range(2):
                        for half in range(2):
                            ridx = (var * 3 + mi) * 2 + half
                            if i == 0:
                                tensor.wait_ge(scst, 1)
                            tensor.wait_ge(swc, 8 * (mi + 1))
                            if i >= 1:
                                tensor.wait_ge(srowc, i)
                            last = None
                            for kc in range(KC):
                                last = tensor.matmul(
                                    p_st[0:1, 0:512],
                                    bmix_bf[:, (var * 3 + mi) * KC + kc:
                                            (var * 3 + mi) * KC + kc + 1],
                                    wmats[mi][:, kc * D + half * 512:
                                              kc * D + half * 512 + 512],
                                    start=(kc == 0), stop=(kc == KC - 1))
                            last.then_inc(srowm, 1)
                            i += 1
                for g in range(NG):
                    c0, c1, a0, a1 = grange(g)
                    sw = a1 - a0
                    cw = c1 - c0
                    xb = (g % 2) * KC * SW
                    tensor.wait_ge(sxg[g], 16 * KC)
                    if g == 0:
                        tensor.wait_ge(srowc, 12)
                    if g >= 1:
                        tensor.wait_ge(sx1, KC * g)
                    for c in range(KC):
                        tensor.matmul(p_st[:, 0:sw], g_ones[:],
                                      xa_g[:, xb + c * SW:xb + c * SW + sw],
                                      start=(c == 0), stop=(c == KC - 1))
                    for c in range(KC):
                        tensor.wait_ge(ssq, KC * g + c + 1)
                        tensor.matmul(
                            p_st[:, 512:512 + sw], g_ones[:],
                            sq_t[:, (c % 2) * SW:(c % 2) * SW + sw],
                            start=(c == 0), stop=(c == KC - 1)
                            ).then_inc(ssmm, 1)
                    for c in range(KC):
                        if c == 0:
                            tensor.wait_ge(svar, g + 1)
                            tensor.wait_ge(ssq2, KC * g + 1)
                        tensor.matmul(p_st[:, 0:sw], g_ones[:],
                                      xo_g[:, c * SW:c * SW + sw],
                                      start=(c == 0), stop=(c == KC - 1))
                    for c in range(KC):
                        tensor.wait_ge(ssq2, KC * g + c + 1)
                        tensor.matmul(
                            p_st[:, 512:512 + sw], g_ones[:],
                            sq_t[:, (c % 2) * SW:(c % 2) * SW + sw],
                            start=(c == 0), stop=(c == KC - 1)
                            ).then_inc(ssm2, 1)
                    for mi, msem in enumerate((smk, smv, smr)):
                        bank = (0, 512, 0)[mi]
                        for oc in range(KC):
                            if mi == 0:
                                if oc == 0 and g >= 1:
                                    tensor.wait_ge(srg, KC * g)
                                if oc >= 1:
                                    tensor.wait_ge(sek, KC * g + oc)
                            if mi == 1:
                                tensor.wait_ge(sekv, KC * g + oc)
                            if mi == 2:
                                if oc == 0:
                                    tensor.wait_ge(sek, KC * (g + 1))
                                else:
                                    tensor.wait_ge(srg, KC * g + oc)
                            for kc in range(KC):
                                if oc == 0:
                                    tensor.wait_ge(
                                        smix, 3 * KC * g + mi * KC + kc + 1)
                                tensor.matmul(
                                    p_kv[:, bank:bank + cw],
                                    wmats[mi][:, kc * D + oc * 128:
                                              kc * D + oc * 128 + 128],
                                    mx_bf[mi][:, kc * G:kc * G + cw],
                                    start=(kc == 0), stop=False)
                            rb = (0 * 3 + mi) * 2
                            ro = (1 * 3 + mi) * 2
                            tensor.matmul(
                                p_kv[:, bank:bank + cw],
                                brows[0:1, rb * 512 + oc * 128:
                                      rb * 512 + oc * 128 + 128],
                                ones_row[0:1, 0:cw],
                                start=False, stop=False)
                            rhs2 = xp0_row if g == 0 else ones_row
                            tensor.matmul(
                                p_kv[:, bank:bank + cw],
                                brows[0:1, ro * 512 + oc * 128:
                                      ro * 512 + oc * 128 + 128],
                                rhs2[0:1, 0:cw],
                                start=False, stop=True).then_inc(msem, 1)
                    if g == 0:
                        tensor.wait_ge(swc, 32)
                    for half in range(2):
                        for dc in range(4 * half, 4 * half + 4):
                            last = None
                            for oc in range(KC):
                                if dc == 0:
                                    tensor.wait_ge(srw, KC * g + oc + 1)
                                last = tensor.matmul(
                                    p_o[:, (dc % 4) * 512:(dc % 4) * 512 + cw],
                                    wo_sb[:, oc * D + dc * 128:
                                          oc * D + dc * 128 + 128],
                                    rw_bf[:, oc * G:oc * G + cw],
                                    start=(oc == 0), stop=(oc == KC - 1))
                            last.then_inc(smo, 1)
                        if half == 0:
                            tensor.wait_ge(sx1, KC * g + 4)

            @blk.scalar
            def _(scalar):
                for g in range(NG):
                    c0, c1, a0, a1 = grange(g)
                    sw = a1 - a0
                    cw = c1 - c0
                    scalar.wait_ge(svar, g + 1)
                    scalar.activation(rstd[:, 0:sw], var_t[:, 0:sw], ACT.Ln,
                                      bias=env["eps"][:, 0:1])
                    scalar.drain()
                    scalar.activation(rstd[:, 0:sw], rstd[:, 0:sw], ACT.Exp,
                                      scale=-0.5).then_inc(srs, 1)
                    scalar.wait_ge(svar2, g + 1)
                    scalar.activation(rstd2[:, 0:sw], var_t[:, 0:sw], ACT.Ln,
                                      bias=env["eps"][:, 0:1])
                    scalar.drain()
                    scalar.activation(rstd2[:, 0:sw], rstd2[:, 0:sw],
                                      ACT.Exp, scale=-0.5).then_inc(srs, 1)
                    for oc in range(KC):
                        scalar.wait_ge(smk, KC * g + oc + 1)
                        if g >= 1:
                            scalar.wait_ge(srw, KC * (g - 1) + oc + 1)
                        if c0 < LH:
                            pcut = min(LH - c0, cw)
                            scalar.activation(ek_g[:, oc * G:oc * G + pcut],
                                              p_kv[:, 0:pcut], ACT.Exp,
                                              bias=env["pen"][:, 0:1])
                            last = scalar.activation(
                                ek_g[:, oc * G + pcut:oc * G + cw],
                                p_kv[:, pcut:cw], ACT.Exp)
                        else:
                            last = scalar.activation(
                                ek_g[:, oc * G:oc * G + cw],
                                p_kv[:, 0:cw], ACT.Exp)
                        if c0 <= LH < c1:
                            j = LH - c0
                            scalar.drain()
                            last = scalar.activation(
                                ek_g[:, oc * G + j:oc * G + j + 1],
                                p_kv[:, j:j + 1], ACT.Exp,
                                bias=env["tfv"][:, oc:oc + 1])
                        last.then_inc(sek, 1)
                    for oc in range(KC):
                        scalar.wait_ge(sscn, KC * g + oc + 1)
                        scalar.activation(winv_t[:, 0:cw],
                                          B_g[:, oc * G:oc * G + cw],
                                          ACT.Ln, bias=epsw[:, 0:1])
                        scalar.drain()
                        scalar.activation(ek_g[:, oc * G:oc * G + cw],
                                          winv_t[:, 0:cw], ACT.Exp,
                                          scale=-1.0).then_inc(slnb, 1)
                        scalar.drain()
                    scalar.wait_ge(scar, g + 1)
                    for oc in range(KC):
                        scalar.wait_ge(smr, KC * g + oc + 1)
                        scalar.activation(B_g[:, oc * G:oc * G + cw],
                                          p_kv[:, 0:cw], ACT.Sigmoid
                                          ).then_inc(srg, 1)

            @blk.gpsimd
            def _(gpsimd):
                for g in range(NG):
                    c0, c1, a0, a1 = grange(g)
                    cw = c1 - c0
                    off = c0 - a0
                    for dc in range(KC):
                        gpsimd.wait_ge(sx1, KC * g + dc + 1)
                        gpsimd.dma_start(
                            out=scr["x1"].ap()[dc, :, c0:c1],
                            in_=xo_g[:, dc * SW + off:dc * SW + off + cw]
                            ).then_inc(sxsg[g], 16)
                for g in range(NG):
                    gpsimd.wait_ge(sxsg[g], 16 * KC)
                if env["debug"]:
                    for c in range(KC):
                        gpsimd.dma_start(out=env["ddbg"]["dx1"].ap()[c, :, :],
                                         in_=scr["x1"].ap()[c, :, :]
                                         ).then_inc(snk, 16)
                    gpsimd.wait_ge(snk, 16 * KC)


def _emit_pb2(env):
    nc = env["nc"]
    din = env["din"]
    scr = env["scr"]
    AX2 = env["ops"]["ANT_AXPY2"]
    g_ones = env["ones"]
    from contextlib import ExitStack

    G2 = GB2
    groups = _grp_ranges(LH - 1, LH + TOWN + 1, G2)
    NG = len(groups)
    with ExitStack() as ctx:
        SB = lambda name, shape, dt=F32: ctx.enter_context(
            nc.sbuf_tensor("b2" + name, list(shape), dt))
        PS = lambda name, shape: ctx.enter_context(
            nc.psum_tensor("b2" + name, list(shape), F32))
        sem = lambda name: ctx.enter_context(nc.semaphore("b2" + name))

        SW2 = G2 + 1
        wstg = SB("wst", (128, 2 * 1024))
        wfk_sb = SB("wfk", (128, KC * HD), BF16)
        wfv_sb = SB("wfv", (128, FCH * D), BF16)
        wfr_sb = SB("wfr", (128, KC * D), BF16)
        x1_g = SB("x1", (128, KC * SW2))
        xa_g = SB("xa", (128, KC * G2))
        core_g = SB("cor", (128, KC * SW2))
        ym_bf = SB("ym", (128, KC * G2), BF16)
        yn_bf = SB("yn", (128, KC * G2), BF16)
        kk_g = SB("kk", (128, FCH * G2), BF16)
        rr_t = SB("rr", (128, 2 * G2))
        rl_t = SB("rl", (128, 2 * G2))
        sq_t = SB("sq", (128, 2 * SW2))
        m_bc = SB("mbc", (128, SW2))
        rstd = SB("rsd", (128, SW2))
        var_t = SB("var", (128, SW2))
        hg2_t = SB("hg2", (128, 2 * KC))
        g2_t = SB("g2c", (128, KC))
        b2_bf = SB("b2b", (128, 2 * KC), BF16)
        b2_f = SB("b2f", (128, 2 * KC))
        brow_fk = SB("bfk", (1, HD), BF16)
        brow_fr = SB("bfr", (1, D), BF16)
        ones_row = SB("onr", (1, G2), BF16)
        xp0_row = SB("xpr", (1, G2), BF16)

        p_st = PS("pst", (128, 1024))
        p_kk = PS("pkk", (128, 1024))
        p_y = PS("py", (128, 2048))

        NLD = 4 * KC + FCH + KC
        swl = [sem("wl%d" % i) for i in range(6)]
        swc = sem("wc")
        scst = sem("cst")
        srowm = sem("rwm"); srowc = sem("rwc")
        sxg = [sem("xg%d" % i) for i in range(NG)]
        sxga = [sem("xa%d" % i) for i in range(NG)]
        sxsg = [sem("xs%d" % i) for i in range(NG)]
        ssq = sem("ssq"); ssmm = sem("smm"); svar = sem("sva"); srs = sem("srs")
        smix = sem("smx")
        smfkp = [sem("fka"), sem("fkb")]
        skkp = [sem("kka"), sem("kkb")]
        smm2 = sem("mm2")
        smr2 = sem("mr2")
        srr = sem("srr")
        sxb = sem("sxb")
        snk = sem("snk")

        def grange(g):
            c0, c1 = groups[g]
            return c0, c1, c0 - 1, c1

        def wsrc(j):
            if j < 4 * KC:
                return din["wfk"].ap()[j // 4, :, (j % 4) * 1024:
                                       (j % 4) * 1024 + 1024]
            if j < 4 * KC + FCH:
                return din["wfv"].ap()[j - 4 * KC, :, :]
            return din["wfr"].ap()[j - 4 * KC - FCH, :, :]

        def wdst(j):
            if j < 4 * KC:
                return wfk_sb[:, (j // 4) * HD + (j % 4) * 1024:
                              (j // 4) * HD + (j % 4) * 1024 + 1024]
            if j < 4 * KC + FCH:
                jj = j - 4 * KC
                return wfv_sb[:, jj * D:(jj + 1) * D]
            jj = j - 4 * KC - FCH
            return wfr_sb[:, jj * D:(jj + 1) * D]

        with nc.Block("b2") as blk:
            @blk.sync
            def _(sync):
                for j in range(NLD):
                    if j >= 2:
                        sync.wait_ge(swc, j - 1)
                    sync.dma_start(
                        out=wstg[:, (j % 2) * 1024:(j % 2) * 1024 + 1024],
                        in_=wsrc(j)).then_inc(swl[j % 6], 16)
                for g in range(NG):
                    c0, c1, a0, a1 = grange(g)
                    if g >= 1:
                        sync.wait_ge(sxb, KC * g)
                    for c in range(KC):
                        sync.dma_start(
                            out=x1_g[:, c * SW2:c * SW2 + (a1 - a0)],
                            in_=scr["x1"].ap()[c, :, a0:a1]
                            ).then_inc(sxg[g], 16)
                    for c in range(KC):
                        sync.dma_start(
                            out=xa_g[:, c * G2:c * G2 + (c1 - c0)],
                            in_=scr["xa"].ap()[c, :, c0:c1]
                            ).then_inc(sxga[g], 16)

            @blk.vector
            def _(vector):
                vector.memset(ones_row[0:1, :], 1.0)
                vector.memset(xp0_row[0:1, :], 1.0)
                vector.drain()
                vector.tensor_scalar(xp0_row[0:1, 1:2], xp0_row[0:1, 1:2],
                                     env["xp0"][0:1, 0:1], None,
                                     ALU.mult, ALU.bypass)
                vector.tensor_scalar(hg2_t[:, 0:KC], env["ln2g"][:],
                                     0.5, None, ALU.mult, ALU.bypass)
                vector.tensor_scalar(b2_f[:, 0:KC], env["ln2b"][:],
                                     0.5, None, ALU.mult, ALU.bypass)
                vector.tensor_copy(g2_t[:, 0:KC], env["ln2g"][:])
                vector.tensor_copy(b2_f[:, KC:2 * KC], env["ln2b"][:])
                vector.drain()
                vector.tensor_scalar(hg2_t[:, KC:2 * KC], hg2_t[:, 0:KC],
                                     env["xp0"][:, 0:1], None,
                                     ALU.mult, ALU.bypass)
                vector.tensor_copy(b2_bf[:, :], b2_f[:, :]).then_inc(scst, 1)
                for j in range(NLD):
                    vector.wait_ge(swl[j % 6], 16 * (j // 6 + 1))
                    vector.tensor_copy(
                        wdst(j),
                        wstg[:, (j % 2) * 1024:(j % 2) * 1024 + 1024]
                        ).then_inc(swc, 1)
                for i in range(10):
                    vector.wait_ge(srowm, i + 1)
                    dst = (brow_fk[0:1, i * 512:(i + 1) * 512] if i < 8
                           else brow_fr[0:1, (i - 8) * 512:(i - 7) * 512])
                    vector.tensor_copy(dst, p_st[0:1, 0:512]
                                       ).then_inc(srowc, 1)
                for g in range(NG):
                    c0, c1, a0, a1 = grange(g)
                    sw = a1 - a0
                    cw = c1 - c0
                    vector.wait_ge(sxg[g], 16 * KC)
                    for c in range(KC):
                        if c >= 2:
                            vector.wait_ge(ssmm, KC * g + c - 1)
                        vector.tensor_tensor(
                            sq_t[:, (c % 2) * SW2:(c % 2) * SW2 + sw],
                            x1_g[:, c * SW2:c * SW2 + sw],
                            x1_g[:, c * SW2:c * SW2 + sw],
                            ALU.mult).then_inc(ssq, 1)
                    if g >= 1:
                        vector.wait_ge(srs, g)
                    vector.wait_ge(ssmm, KC * (g + 1))
                    vector.tensor_scalar(m_bc[:, 0:sw], p_st[:, 0:sw],
                                         1.0 / D, None, ALU.mult, ALU.bypass)
                    vector.drain()
                    vector.tensor_tensor(var_t[:, 0:sw], m_bc[:, 0:sw],
                                         m_bc[:, 0:sw], ALU.mult)
                    vector.drain()
                    vector.scalar_tensor_tensor(
                        var_t[:, 0:sw], p_st[:, 512:512 + sw], 1.0 / D,
                        var_t[:, 0:sw], ALU.mult, ALU.subtract
                        ).then_inc(svar, 1)
                    vector.wait_ge(sxga[g], 16 * KC)
                    for c in range(KC):
                        vector.tensor_tensor(
                            xa_g[:, c * G2:c * G2 + cw],
                            x1_g[:, c * SW2 + 1:c * SW2 + 1 + cw],
                            xa_g[:, c * G2:c * G2 + cw], ALU.add)
                    vector.drain()
                    vector.wait_ge(srs, g + 1)
                    if g >= 1:
                        vector.wait_ge(sxsg[g - 1], 16 * KC)
                    for c in range(KC):
                        vector.tensor_tensor(
                            core_g[:, c * SW2:c * SW2 + sw],
                            x1_g[:, c * SW2:c * SW2 + sw],
                            m_bc[:, 0:sw], ALU.subtract)
                    vector.drain()
                    for c in range(KC):
                        vector.tensor_tensor(
                            core_g[:, c * SW2:c * SW2 + sw],
                            core_g[:, c * SW2:c * SW2 + sw],
                            rstd[:, 0:sw], ALU.mult)
                    vector.drain()
                    if g >= 1:
                        vector.wait_ge(smfkp[(g - 1) % 2],
                                       FCH * ((g - 1) // 2 + 1))
                    for c in range(KC):
                        hg2 = hg2_t[:, c:c + 1]
                        last = vector._custom_dve(
                            AX2, out=ym_bf[:, c * G2:c * G2 + cw],
                            in0=core_g[:, c * SW2 + 1:c * SW2 + 1 + cw],
                            in1=core_g[:, c * SW2:c * SW2 + cw],
                            s0=hg2, s1=hg2)
                        if g == 0:
                            j = LH - c0
                            vector.drain()
                            last = vector._custom_dve(
                                AX2, out=ym_bf[:, c * G2 + j:c * G2 + j + 1],
                                in0=core_g[:, c * SW2 + 1 + j:
                                           c * SW2 + 2 + j],
                                in1=core_g[:, c * SW2 + j:c * SW2 + 1 + j],
                                s0=hg2, s1=hg2_t[:, KC + c:KC + c + 1])
                        last.then_inc(smix, 1)
                    if g >= 1:
                        vector.wait_ge(smr2, KC * g)
                    for c in range(KC):
                        vector.tensor_scalar(
                            yn_bf[:, c * G2:c * G2 + cw],
                            core_g[:, c * SW2 + 1:c * SW2 + 1 + cw],
                            g2_t[:, c:c + 1], None, ALU.mult, ALU.bypass
                            ).then_inc(smix, 1)
                    for dc in range(KC):
                        vector.wait_ge(smm2, KC * g + dc + 1)
                        vector.wait_ge(srr, KC * g + dc + 1)
                        vector.tensor_tensor(
                            core_g[:, dc * SW2 + 1:dc * SW2 + 1 + cw],
                            p_y[:, (dc % 4) * 512:(dc % 4) * 512 + cw],
                            rr_t[:, (dc % 2) * G2:(dc % 2) * G2 + cw],
                            ALU.mult)
                        vector.drain()
                        vector.tensor_tensor(
                            core_g[:, dc * SW2 + 1:dc * SW2 + 1 + cw],
                            core_g[:, dc * SW2 + 1:dc * SW2 + 1 + cw],
                            xa_g[:, dc * G2:dc * G2 + cw],
                            ALU.add).then_inc(sxb, 1)

            @blk.tensor
            def _(tensor):
                for i in range(10):
                    if i == 0:
                        tensor.wait_ge(scst, 1)
                        tensor.wait_ge(swc, 4 * KC)
                    if i == 8:
                        tensor.wait_ge(swc, NLD)
                    if i >= 1:
                        tensor.wait_ge(srowc, i)
                    last = None
                    for kc in range(KC):
                        src = (wfk_sb[:, kc * HD + i * 512:
                                      kc * HD + i * 512 + 512] if i < 8
                               else wfr_sb[:, kc * D + (i - 8) * 512:
                                           kc * D + (i - 8) * 512 + 512])
                        col = 0 if i < 8 else KC
                        last = tensor.matmul(p_st[0:1, 0:512],
                                             b2_bf[:, col + kc:col + kc + 1],
                                             src, start=(kc == 0),
                                             stop=(kc == KC - 1))
                    last.then_inc(srowm, 1)
                for g in range(NG):
                    c0, c1, a0, a1 = grange(g)
                    sw = a1 - a0
                    cw = c1 - c0
                    tensor.wait_ge(sxg[g], 16 * KC)
                    if g == 0:
                        tensor.wait_ge(srowc, 10)
                    if g >= 1:
                        tensor.wait_ge(sxb, KC * g)
                        tensor.wait_ge(srr, KC * g)
                    for c in range(KC):
                        tensor.matmul(p_st[:, 0:sw], g_ones[:],
                                      x1_g[:, c * SW2:c * SW2 + sw],
                                      start=(c == 0), stop=(c == KC - 1))
                    for c in range(KC):
                        tensor.wait_ge(ssq, KC * g + c + 1)
                        tensor.matmul(
                            p_st[:, 512:512 + sw], g_ones[:],
                            sq_t[:, (c % 2) * SW2:(c % 2) * SW2 + sw],
                            start=(c == 0), stop=(c == KC - 1)
                            ).then_inc(ssmm, 1)
                    for fc in range(FCH):
                        if fc >= 2:
                            tensor.wait_ge(skkp[g % 2],
                                           FCH * (g // 2) + fc - 1)
                        for kc in range(KC):
                            if fc == 0:
                                tensor.wait_ge(smix, 16 * g + kc + 1)
                            tensor.matmul(
                                p_kk[:, (fc % 2) * 512:(fc % 2) * 512 + cw],
                                wfk_sb[:, kc * HD + fc * 128:
                                       kc * HD + fc * 128 + 128],
                                ym_bf[:, kc * G2:kc * G2 + cw],
                                start=(kc == 0), stop=False)
                        tensor.matmul(
                            p_kk[:, (fc % 2) * 512:(fc % 2) * 512 + cw],
                            brow_fk[0:1, fc * 128:fc * 128 + 128],
                            ones_row[0:1, 0:cw], start=False, stop=False)
                        rhs2 = xp0_row if g == 0 else ones_row
                        tensor.matmul(
                            p_kk[:, (fc % 2) * 512:(fc % 2) * 512 + cw],
                            brow_fk[0:1, fc * 128:fc * 128 + 128],
                            rhs2[0:1, 0:cw], start=False, stop=True
                            ).then_inc(smfkp[g % 2], 1)
                    for half in range(2):
                        for dc in range(4 * half, 4 * half + 4):
                            last = None
                            for fc in range(FCH):
                                if dc == 0:
                                    tensor.wait_ge(skkp[g % 2],
                                                   FCH * (g // 2) + fc + 1)
                                last = tensor.matmul(
                                    p_y[:, (dc % 4) * 512:
                                        (dc % 4) * 512 + cw],
                                    wfv_sb[:, fc * D + dc * 128:
                                           fc * D + dc * 128 + 128],
                                    kk_g[:, fc * G2:fc * G2 + cw],
                                    start=(fc == 0), stop=(fc == FCH - 1))
                            last.then_inc(smm2, 1)
                        for dc in range(4 * half, 4 * half + 4):
                            if dc == 0:
                                tensor.wait_ge(svar, g + 1)
                            if dc >= 2:
                                tensor.wait_ge(srr, KC * g + dc - 1)
                            for kc in range(KC):
                                if dc == 0:
                                    tensor.wait_ge(smix,
                                                   16 * g + 8 + kc + 1)
                                tensor.matmul(
                                    p_st[:, (dc % 2) * 512:
                                         (dc % 2) * 512 + cw],
                                    wfr_sb[:, kc * D + dc * 128:
                                           kc * D + dc * 128 + 128],
                                    yn_bf[:, kc * G2:kc * G2 + cw],
                                    start=(kc == 0), stop=False)
                            tensor.matmul(
                                p_st[:, (dc % 2) * 512:(dc % 2) * 512 + cw],
                                brow_fr[0:1, dc * 128:dc * 128 + 128],
                                ones_row[0:1, 0:cw], start=False, stop=True
                                ).then_inc(smr2, 1)
                        if half == 0:
                            tensor.wait_ge(sxb, KC * g + 4)

            @blk.scalar
            def _(scalar):
                for g in range(NG):
                    c0, c1, a0, a1 = grange(g)
                    sw = a1 - a0
                    cw = c1 - c0
                    scalar.wait_ge(svar, g + 1)
                    scalar.activation(rstd[:, 0:sw], var_t[:, 0:sw], ACT.Ln,
                                      bias=env["eps"][:, 0:1])
                    scalar.drain()
                    scalar.activation(rstd[:, 0:sw], rstd[:, 0:sw], ACT.Exp,
                                      scale=-0.5).then_inc(srs, 1)
                    for fc in range(FCH):
                        scalar.wait_ge(smfkp[g % 2],
                                       FCH * (g // 2) + fc + 1)
                        if g >= 1 and fc == 0:
                            scalar.wait_ge(smm2, KC * g)
                        scalar.activation(
                            rl_t[:, (fc % 2) * G2:(fc % 2) * G2 + cw],
                            p_kk[:, (fc % 2) * 512:(fc % 2) * 512 + cw],
                            ACT.Relu)
                        scalar.drain()
                        scalar.activation(
                            kk_g[:, fc * G2:fc * G2 + cw],
                            rl_t[:, (fc % 2) * G2:(fc % 2) * G2 + cw],
                            ACT.Square).then_inc(skkp[g % 2], 1)
                        scalar.drain()
                    for dc in range(KC):
                        scalar.wait_ge(smr2, KC * g + dc + 1)
                        if g >= 1 or dc >= 2:
                            scalar.wait_ge(sxb, KC * g + dc - 1)
                        scalar.activation(
                            rr_t[:, (dc % 2) * G2:(dc % 2) * G2 + cw],
                            p_st[:, (dc % 2) * 512:(dc % 2) * 512 + cw],
                            ACT.Sigmoid).then_inc(srr, 1)

            @blk.gpsimd
            def _(gpsimd):
                for g in range(NG):
                    c0, c1, a0, a1 = grange(g)
                    cw = c1 - c0
                    for dc in range(KC):
                        gpsimd.wait_ge(sxb, KC * g + dc + 1)
                        gpsimd.dma_start(
                            out=scr["xb"].ap()[dc, :, c0:c1],
                            in_=core_g[:, dc * SW2 + 1:dc * SW2 + 1 + cw]
                            ).then_inc(sxsg[g], 16)
                for g in range(NG):
                    gpsimd.wait_ge(sxsg[g], 16 * KC)
                if env["debug"]:
                    for c in range(KC):
                        gpsimd.dma_start(
                            out=env["ddbg"]["dxb"].ap()[c, :,
                                                        LH - 1:LH + TOWN + 1],
                            in_=scr["xb"].ap()[c, :, LH - 1:LH + TOWN + 1]
                            ).then_inc(snk, 16)
                    gpsimd.wait_ge(snk, 16 * KC)


def _emit_tail(env, phases):
    if "PC" in phases:
        return
    nc = env["nc"]
    with nc.semaphore("tlv") as sv, nc.semaphore("tl") as st, \
            nc.sbuf_tensor("tl0", [128, TOWN], F32) as zt, \
            nc.Block("tail") as blk:
        @blk.vector
        def _(vector):
            vector.memset(zt[:], 0.0).then_inc(sv, 1)

        @blk.gpsimd
        def _(gpsimd):
            gpsimd.wait_ge(sv, 1)
            for c in range(KC):
                gpsimd.dma_start(out=env["d_out"].ap()[c, :, :], in_=zt[:]
                                 ).then_inc(st, 16)
            gpsimd.wait_ge(st, 16 * KC)


def _host_inputs(inputs, core):
    x = np.asarray(inputs["L0_attn_x"])[0]
    T = x.shape[0]
    s = core * TOWN
    lo, hi = s - LH, s + TOWN + RH
    pad_l, pad_r = max(0, -lo), max(0, hi - T)
    seg = x[max(lo, 0):min(hi, T)]
    if pad_l or pad_r:
        seg = np.pad(seg, ((pad_l, pad_r), (0, 0)))
    xfm = np.ascontiguousarray(seg.T.reshape(KC, 128, TE))

    def chunkT(w):
        w = np.ascontiguousarray(w)
        return np.ascontiguousarray(w.reshape(-1, 128, w.shape[1]))

    def vec8(v):
        return np.ascontiguousarray(np.asarray(v).reshape(KC, 128).T)

    z1 = np.zeros((128, 1), np.float32)
    o1 = np.ones((128, 1), np.float32)
    return {
        "xfm": xfm,
        "w1t": chunkT(np.asarray(inputs["L0_ffn_w1"]).T),
        "w2t": chunkT(np.asarray(inputs["L0_ffn_w2"]).T),
        "wk": chunkT(np.asarray(inputs["L0_attn_Wk"])),
        "wv": chunkT(np.asarray(inputs["L0_attn_Wv"])),
        "wr": chunkT(np.asarray(inputs["L0_attn_Wr"])),
        "wo": chunkT(np.asarray(inputs["L0_attn_Wo"])),
        "wfr": chunkT(np.asarray(inputs["L0_attn_Wfr"])),
        "wfk": chunkT(np.asarray(inputs["L0_attn_Wfk"])),
        "wfv": chunkT(np.asarray(inputs["L0_attn_Wfv"])),
        "convw": np.ascontiguousarray(
            np.asarray(inputs["L0_ffn_conv_w"])[:, 0, :]
            .reshape(FCH, 128, 3).transpose(1, 0, 2).reshape(128, 3 * FCH)),
        "convb": np.ascontiguousarray(
            np.asarray(inputs["L0_ffn_conv_b"]).reshape(FCH, 128).T),
        "ln1g": vec8(inputs["L0_attn_ln1_g"]),
        "ln1b": vec8(inputs["L0_attn_ln1_b"]),
        "td": vec8(inputs["L0_attn_time_decay"]),
        "mixk": vec8(inputs["L0_attn_mix_k"]),
        "mixv": vec8(inputs["L0_attn_mix_v"]),
        "mixr": vec8(inputs["L0_attn_mix_r"]),
        "ln2g": vec8(inputs["L0_attn_ln2_g"]),
        "ln2b": vec8(inputs["L0_attn_ln2_b"]),
        "tfv": (vec8(inputs["L0_attn_time_first"]) if core == 0
                else np.zeros((128, KC), np.float32)),
        "pen": (np.full((128, 1), -1000.0, np.float32) if core == 0 else z1),
        "xp0": (z1 if core == 0 else o1),
        "eml": (z1 if core == 0 else o1),
        "emr": (z1 if core == NCORE - 1 else o1),
        "ident": np.eye(128, dtype=np.float32),
    }


def kernel(**inputs):
    nc = build(debug=False)
    lower_extended_insts(nc)
    in_maps = [_host_inputs(inputs, c) for c in range(NCORE)]
    res = run_bass_kernel_spmd(nc, in_maps, core_ids=list(range(NCORE)))
    T = np.asarray(inputs["L0_attn_x"]).shape[1]
    out = np.empty((1, T, D), np.float32)
    for c in range(NCORE):
        o = np.asarray(res.results[c]["out"])
        out[0, c * TOWN:(c + 1) * TOWN, :] = o.reshape(D, TOWN).T
    return out
